# version 25
# baseline (speedup 1.0000x reference)
import numpy as np
import ml_dtypes
from contextlib import ExitStack

import concourse.bass as bass
import concourse.mybir as mybir
from concourse.bass_utils import run_bass_kernel_spmd

F32 = mybir.dt.float32
BF16 = mybir.dt.bfloat16
AF = mybir.ActivationFunctionType
ALU = mybir.AluOpType
AX = mybir.AxisListType

SEQ, D, H, DK = 2048, 1024, 16, 64
NT = SEQ // 128
FT = D // 128
PAIRS = FT
NCH = SEQ // 512
DEC_W = 2432


class Sem:
    def __init__(self, stack, nc, name):
        self.h = stack.enter_context(nc.semaphore(name))
        self.v = 0


class Plan:
    def __init__(self):
        self.ops = {k: [] for k in ("sync", "gpsimd", "vector", "scalar", "tensor")}

    def emit(self, eng, build_fn, waits=(), inc=None, k=1):
        w = tuple((s.h, int(v)) for (s, v) in waits if s is not None)
        if inc is not None:
            inc.v += k
        def op(e, w=w, build_fn=build_fn, inc=inc, k=k):
            for (h, val) in w:
                e.wait_ge(h, val)
            ins = build_fn(e)
            if inc is not None:
                ins.then_inc(inc.h, k)
        self.ops[eng].append(op)
        return inc.v if inc is not None else None


class PsumPool:
    def __init__(self, tensors):
        self.t = tensors
        self.free_req = [None] * len(tensors)
        self.i = 0

    def acquire(self):
        j = self.i % len(self.t)
        self.i += 1
        req = self.free_req[j]
        return j, ([] if req is None else [req])

    def release(self, j, sem, val):
        self.free_req[j] = (sem, val)


DEBUG_TAPS = False
PARANOID = False
INTERLEAVE_QK = True
QUARTER_NORM = False


def build_program():
    nc = bass.Bass("TRN2", target_bir_lowering=False)

    dx = nc.dram_tensor("x", [SEQ, D], F32, kind="ExternalInput").ap()
    dwq = nc.dram_tensor("wq", [D, D], F32, kind="ExternalInput").ap()
    dwk = nc.dram_tensor("wk", [D, D], F32, kind="ExternalInput").ap()
    dwv = nc.dram_tensor("wv", [D, D], F32, kind="ExternalInput").ap()
    dwo = nc.dram_tensor("wo", [D, D], F32, kind="ExternalInput").ap()
    dbq = nc.dram_tensor("bq", [D], F32, kind="ExternalInput").ap()
    dbk = nc.dram_tensor("bk", [D], F32, kind="ExternalInput").ap()
    dbv = nc.dram_tensor("bv", [D], F32, kind="ExternalInput").ap()
    dbo = nc.dram_tensor("bo", [1, D], F32, kind="ExternalInput").ap()
    dbf = nc.dram_tensor("bfn", [1, D], F32, kind="ExternalInput").ap()
    dg1 = nc.dram_tensor("g1rep", [8, 512, 512], F32, kind="ExternalInput").ap()
    dg2 = nc.dram_tensor("g2rep", [8, 512, 512], F32, kind="ExternalInput").ap()
    dident = nc.dram_tensor("ident", [128, 128], BF16, kind="ExternalInput").ap()
    dgrid = nc.dram_tensor("dgrid", [128, DEC_W], F32, kind="ExternalInput").ap()
    dhsel = nc.dram_tensor("hsel", [128, 2], BF16, kind="ExternalInput").ap()
    dhselT = nc.dram_tensor("hselT", [2, 128], BF16, kind="ExternalInput").ap()
    dout = nc.dram_tensor("out", [SEQ, D], F32, kind="ExternalOutput").ap()
    taps = {}
    if DEBUG_TAPS:
        for nm, shp, dt in [("t_attn", [128, FT * SEQ], F32), ("t_x2", [128, NT * 1024], F32),
                            ("t_xnT", [128, FT * SEQ], F32), ("t_wf", [128, FT * D], F32),
                            ("t_decay", [128, DEC_W], F32), ("t_qt", [128, 2 * SEQ], F32),
                            ("t_kt", [128, 2 * SEQ], F32), ("t_negc", [128, 2], F32),
                            ("t_pt", [128, 2 * 1024], F32), ("t_pvpr", [128, 1024], F32),
                            ("t_v", [128, NT * 1024], F32)]:
            taps[nm] = nc.dram_tensor(nm, shp, dt, kind="ExternalOutput").ap()

    P = Plan()

    with ExitStack() as TOP:
        E = TOP.enter_context
        dS = Sem(TOP, nc, "dS")
        dG = Sem(TOP, nc, "dG")
        dG2 = Sem(TOP, nc, "dG2")
        dGrid = Sem(TOP, nc, "dGrid")
        dIdent = Sem(TOP, nc, "dIdent")
        dHsel = Sem(TOP, nc, "dHsel")
        dBq = Sem(TOP, nc, "dBq")
        dBk = Sem(TOP, nc, "dBk")
        dBo = Sem(TOP, nc, "dBo")
        dBf = Sem(TOP, nc, "dBf")
        dBv = Sem(TOP, nc, "dBv")
        dWv = Sem(TOP, nc, "dWv")
        dWo = Sem(TOP, nc, "dWo")
        dGrep = Sem(TOP, nc, "dGrep")
        dX = [Sem(TOP, nc, "dX0"), Sem(TOP, nc, "dX1"), Sem(TOP, nc, "dX2")]
        dXg = [Sem(TOP, nc, "dXg0"), Sem(TOP, nc, "dXg1"), Sem(TOP, nc, "dXg2")]
        tE = Sem(TOP, nc, "tE")
        vE = Sem(TOP, nc, "vE")
        aE = Sem(TOP, nc, "aE")
        pE = Sem(TOP, nc, "pE")

        attn_outT = E(nc.sbuf_tensor("attn_outT", [128, FT * SEQ], BF16))
        xnT = E(nc.sbuf_tensor("xnT", [128, FT * SEQ], BF16))
        xrot = E(nc.sbuf_tensor("xrot", [128, 3 * 1024], F32))
        xnrot = E(nc.sbuf_tensor("xnrot", [128, 2 * 1024], BF16))
        sqscr = E(nc.sbuf_tensor("sqscr", [128, 1024], F32))
        stat = E(nc.sbuf_tensor("stat", [128, 96], F32))
        identS = E(nc.sbuf_tensor("identS", [128, 128], BF16))
        ones1 = E(nc.sbuf_tensor("ones1", [1, 128], BF16))
        ones64 = E(nc.sbuf_tensor("ones64", [128, 64], BF16))
        hselS = E(nc.sbuf_tensor("hselS", [128, 2], BF16))
        hselTS = E(nc.sbuf_tensor("hselTS", [2, 128], BF16))
        bqS = E(nc.sbuf_tensor("bqS", [128, 8], F32))
        bq8S = E(nc.sbuf_tensor("bq8S", [128, 8], F32))
        bkS = E(nc.sbuf_tensor("bkS", [128, 8], F32))
        bvS = E(nc.sbuf_tensor("bvS", [128, 8], BF16))
        boHi = E(nc.sbuf_tensor("boHi", [1, D], BF16))
        boLo = E(nc.sbuf_tensor("boLo", [1, D], BF16))
        decayS = E(nc.sbuf_tensor("decayS", [128, DEC_W], BF16))

        def ftv(t):
            return t[:, :].rearrange("p (a n) -> p a n", a=FT)

        P.emit("gpsimd", lambda g: g.dma_start(
            out=bvS[:, :], in_=dbv.rearrange("(t p) -> p t", p=128),
            allow_slow_non_contiguous=True), inc=dBv, k=16)
        dg_bv = dBv.v

        maskscope = ExitStack()
        B1 = maskscope.enter_context(nc.sbuf_tensor("B1", [128, DEC_W], F32))
        B2 = maskscope.enter_context(nc.sbuf_tensor("B2", [128, DEC_W], F32))
        P.emit("sync", lambda s: s.dma_start(out=B1[:, :], in_=dgrid), inc=dGrid, k=16)
        ds_grid = dGrid.v
        P.emit("sync", lambda s: s.dma_start(out=identS[:, :], in_=dident), inc=dIdent, k=16)
        ds_ident = dIdent.v
        P.emit("sync", lambda s: s.dma_start(out=hselS[:, :], in_=dhsel), inc=dHsel, k=16)
        P.emit("sync", lambda s: s.dma_start(out=hselTS[:, :], in_=dhselT), inc=dHsel, k=16)
        ds_hsel = dHsel.v
        P.emit("sync", lambda s: s.dma_start(
            out=bqS[:, :], in_=dbq.rearrange("(t p) -> p t", p=128),
            allow_slow_non_contiguous=True), inc=dBq, k=16)
        ds_bq = dBq.v
        P.emit("sync", lambda s: s.dma_start(
            out=bkS[:, :], in_=dbk.rearrange("(t p) -> p t", p=128),
            allow_slow_non_contiguous=True), inc=dBk, k=16)
        ds_bk = dBk.v

        P.emit("vector", lambda v: v.memset(ones1[:, :], 1.0))
        P.emit("vector", lambda v: v.memset(ones64[:, :], 1.0))
        ve_ones = P.emit("vector", lambda v: v.drain(fusable=False), inc=vE)

        mask_state = {}

        def mask_step(step):
            if step == 0:
                m0 = P.emit("vector", lambda v: v.tensor_scalar(
                    B2[:, :], B1[:, :], 0.0, 1e30, ALU.min, ALU.mult),
                    waits=[(dGrid, ds_grid)], inc=vE)
                P.emit("vector", lambda v: v.tensor_scalar_max(
                    B1[:, :], B1[:, :], 1.0), waits=[(vE, m0)])
                mask_state["ve_b1"] = P.emit(
                    "vector", lambda v: v.drain(fusable=False), inc=vE)
            elif step == 1:
                P.emit("scalar", lambda s: s.activation(B1[:, :], B1[:, :], AF.Ln),
                       waits=[(vE, mask_state["ve_b1"])])
                mask_state["ae_ln0"] = P.emit(
                    "scalar", lambda s: s.drain(fusable=False), inc=aE)
            elif step == 2:
                P.emit("vector", lambda v: v.scalar_tensor_tensor(
                    B1[:, :], B1[:, :], -0.25, B2[:, :], ALU.mult, ALU.add),
                    waits=[(aE, mask_state["ae_ln0"])])
                mask_state["ve_strip"] = P.emit(
                    "vector", lambda v: v.drain(fusable=False), inc=vE)
            elif step == 3:
                P.emit("scalar", lambda s: s.activation(
                    decayS[:, :], B1[:, :], AF.Exp),
                    waits=[(vE, mask_state["ve_strip"])])
                mask_state["ae_decay"] = P.emit(
                    "scalar", lambda s: s.drain(fusable=False), inc=aE)

        P.emit("vector", lambda v: v.tensor_scalar_mul(bq8S[:, :], bqS[:, :], 0.125),
               waits=[(dBq, ds_bq)])
        ve_bq8 = P.emit("vector", lambda v: v.drain(fusable=False), inc=vE)
        maskscope.close()

        ln_state = {"prev_r2": None}

        def emit_ln_stats(i, src_ap, src_waits):
            c = i
            w0 = list(src_waits)
            P.emit("vector", lambda v, s=src_ap, c=c: v.reduce_sum(
                stat[:, c:c + 1], s, AX.X), waits=w0, inc=vE)
            sqw = list(src_waits)
            if ln_state["prev_r2"] is not None:
                sqw.append((vE, ln_state["prev_r2"]))
            sq = P.emit("gpsimd", lambda g, s=src_ap: g.tensor_mul(
                sqscr[:, :], s, s), waits=sqw, inc=pE)
            r2 = P.emit("vector", lambda v, c=c: v.reduce_sum(
                stat[:, 16 + c:17 + c], sqscr[:, :], AX.X),
                waits=[(pE, sq)], inc=vE)
            ln_state["prev_r2"] = r2
            nm = P.emit("vector", lambda v, c=c: v.tensor_scalar_mul(
                stat[:, 32 + c:33 + c], stat[:, c:c + 1], -1.0 / D),
                waits=[(vE, r2)], inc=vE)
            P.emit("vector", lambda v, c=c: v.tensor_scalar(
                stat[:, 48 + c:49 + c], stat[:, 16 + c:17 + c], 1.0 / D, 1e-5,
                ALU.mult, ALU.add), inc=vE)
            n2 = P.emit("vector", lambda v, c=c: v.tensor_scalar(
                stat[:, 64 + c:65 + c], stat[:, 32 + c:33 + c],
                stat[:, 32 + c:33 + c], -1.0, ALU.mult, ALU.mult),
                waits=[(vE, nm)], inc=vE)
            P.emit("vector", lambda v, c=c: v.tensor_add(
                stat[:, 48 + c:49 + c], stat[:, 48 + c:49 + c],
                stat[:, 64 + c:65 + c]), waits=[(vE, n2)])
            vr = P.emit("vector", lambda v: v.drain(fusable=False), inc=vE)
            return vr

        def emit_ln_lnact(i, vr):
            c = i
            P.emit("scalar", lambda s, c=c: s.activation(
                stat[:, 80 + c:81 + c], stat[:, 48 + c:49 + c], AF.Ln),
                waits=[(vE, vr)], inc=aE)
            ln_ = aE.v
            P.emit("scalar", lambda s, c=c: s.activation(
                stat[:, 80 + c:81 + c], stat[:, 80 + c:81 + c], AF.Exp, scale=-0.5),
                waits=[(aE, ln_)])
            rs = P.emit("scalar", lambda s: s.drain(fusable=False), inc=aE)
            return rs

        def emit_ln_apply(i, src_ap, rs, apply_extra):
            c = i
            sl = (i % 2) * 1024
            apw = [(aE, rs)] + list(apply_extra)
            P.emit("vector", lambda v, s=src_ap, c=c, sl=sl: v.tensor_scalar(
                xnrot[:, sl:sl + 1024], s, stat[:, 32 + c:33 + c],
                stat[:, 80 + c:81 + c], ALU.add, ALU.mult),
                waits=apw)
            ap = P.emit("vector", lambda v: v.drain(fusable=False), inc=vE)
            return ap

        xslot_free = [None, None, None]
        xnrot_tr_gate = [None, None]
        xe_rec = [0] * NT
        lnst = {}
        lnrs = {}
        with ExitStack() as PS1:
            psT = [PS1.enter_context(nc.psum_tensor("psT0", [128, 1024], BF16)),
                   PS1.enter_context(nc.psum_tensor("psT1", [128, 1024], BF16))]
            psT_free = [None, None]
            ds_x = {}
            xsems = {}
            for i in range(NT + 2):
                if i < NT:
                    sl3 = (i % 3) * 1024
                    wslot = [] if xslot_free[i % 3] is None else [xslot_free[i % 3]]
                    xeng = "gpsimd" if i >= 12 else "sync"
                    xsem = dXg[i % 3] if i >= 12 else dX[i % 3]
                    P.emit(xeng, lambda s, i=i, sl3=sl3: s.dma_start(
                        out=xrot[:, sl3:sl3 + 1024],
                        in_=dx[i * 128:(i + 1) * 128, :]),
                        waits=wslot, inc=xsem, k=16)
                    ds_x[i] = xsem.v
                    xsems[i] = xsem
                if 2 <= i and i - 2 < NT:
                    t = i - 2
                    sl3 = (t % 3) * 1024
                    sl = (t % 2) * 1024
                    extra = ([] if xnrot_tr_gate[t % 2] is None
                             else [xnrot_tr_gate[t % 2]])
                    ap = emit_ln_apply(t, xrot[:, sl3:sl3 + 1024], lnrs[t], extra)
                    xslot_free[t % 3] = (vE, ap)
                    tw = [(vE, ap), (dIdent, ds_ident)]
                    if psT_free[t % 2] is not None:
                        tw.append(psT_free[t % 2])
                    for j in range(FT):
                        P.emit("tensor", lambda t_, t=t, j=j, sl=sl: t_.transpose(
                            psT[t % 2][:, j * 128:(j + 1) * 128],
                            xnrot[:, sl + j * 128:sl + (j + 1) * 128], identS[:, :]),
                            waits=(tw if j == 0 else []),
                            inc=(tE if j == FT - 1 else None))
                    trrec = tE.v
                    xnrot_tr_gate[t % 2] = (tE, trrec)
                    dst = ftv(xnT)[:, :, t * 128:(t + 1) * 128]
                    srcT = psT[t % 2][:, :].rearrange("p (a n) -> p a n", a=FT)
                    xe = P.emit("vector", lambda v, dst=dst, srcT=srcT: v.tensor_copy(
                        dst, srcT), waits=[(tE, trrec)], inc=vE)
                    xe_rec[t] = xe
                    psT_free[t % 2] = (vE, xe)
                    if t in (2, 5, 8, 11):
                        mask_step((2, 5, 8, 11).index(t))
                if 1 <= i and i - 1 < NT:
                    lnrs[i - 1] = emit_ln_lnact(i - 1, lnst[i - 1])
                if i < NT:
                    lnst[i] = emit_ln_stats(
                        i, xrot[:, (i % 3) * 1024:(i % 3) * 1024 + 1024],
                        [(xsems[i], ds_x[i])])
        ae_decay = mask_state["ae_decay"]
        P.emit("vector", lambda v: v.drain(fusable=False), inc=vE)
        xnT_all = vE.v
        xnT_chunk = [(xe_rec[4 * c + 3 + 1] if 4 * c + 4 < NT else xnT_all)
                     for c in range(NCH)]

        WfS = TOP.enter_context(nc.sbuf_tensor("WfS", [128, FT * D], BF16))
        attnscope = ExitStack()
        QTs = attnscope.enter_context(nc.sbuf_tensor("QTs", [128, 2 * SEQ], BF16))
        KTs = attnscope.enter_context(nc.sbuf_tensor("KTs", [128, 2 * SEQ], BF16))
        V = attnscope.enter_context(nc.sbuf_tensor("V", [128, NT * 1024], BF16))
        PTb = attnscope.enter_context(nc.sbuf_tensor("PTb", [128, 4 * 1024], BF16))
        Wqp = attnscope.enter_context(nc.sbuf_tensor("Wqp", [128, 2 * 1024], BF16))
        Wkp = attnscope.enter_context(nc.sbuf_tensor("Wkp", [128, 2 * 1024], BF16))
        cmS = attnscope.enter_context(nc.sbuf_tensor("cmS", [2, 16], F32))
        cmbf = attnscope.enter_context(nc.sbuf_tensor("cmbf", [2, 1], BF16))
        negCS = attnscope.enter_context(nc.sbuf_tensor("negCS", [128, 2], F32))
        recipS = attnscope.enter_context(nc.sbuf_tensor("recipS", [128, 512], F32))
        qsqT = attnscope.enter_context(nc.sbuf_tensor("qsqT", [128, SEQ], BF16))
        qsqbf = qsqT[:, :]

        pscope = ExitStack()
        stP = [pscope.enter_context(nc.psum_tensor("stP0", [128, 1024], F32)),
               pscope.enter_context(nc.psum_tensor("stP1", [128, 1024], F32))]
        qkP = [pscope.enter_context(nc.psum_tensor("qk0", [128, 512], F32)),
               pscope.enter_context(nc.psum_tensor("qk1", [128, 512], F32))]
        pool = PsumPool(qkP)

        wslot_gate = [None, None]
        qkslot_gate = [(aE, ae_decay), (aE, ae_decay)]
        dg_w = [None] * PAIRS

        def emit_qk_weights(p):
            pi = p % 2
            wg = [] if wslot_gate[pi] is None else [wslot_gate[pi]]
            src_q = dwq.rearrange("(t p) f -> p t f", p=128)[:, :, p * 128:(p + 1) * 128]
            src_k = dwk.rearrange("(t p) f -> p t f", p=128)[:, :, p * 128:(p + 1) * 128]
            sem_w = dG if pi == 0 else dG2
            P.emit("gpsimd", lambda g, pi=pi, src_q=src_q: g.dma_start(
                out=Wqp[:, pi * 1024:(pi + 1) * 1024].rearrange(
                    "p (t f) -> p t f", t=FT), in_=src_q),
                waits=wg, inc=sem_w, k=16)
            P.emit("gpsimd", lambda g, pi=pi, src_k=src_k: g.dma_start(
                out=Wkp[:, pi * 1024:(pi + 1) * 1024].rearrange(
                    "p (t f) -> p t f", t=FT), in_=src_k),
                inc=sem_w, k=16)
            dg_w[p] = (sem_w, sem_w.v)

        def emit_qk_compute(p):
            pi = p % 2
            last_ev = 0
            last_mm = 0
            for (which, Wt) in (("q", Wqp), ("k", Wkp)):
                for c in range(NCH):
                    j, pw = pool.acquire()
                    for tin in range(FT):
                        w = []
                        if tin == 0:
                            w = [(vE, xnT_chunk[c]), dg_w[p]] + pw
                            if qkslot_gate[pi] is not None:
                                w.append(qkslot_gate[pi])
                        P.emit("tensor", lambda t, j=j, Wt=Wt, tin=tin, pi=pi, c=c:
                               t.matmul(qkP[j][:, 0:512],
                                        Wt[:, pi * 1024 + tin * 128:pi * 1024 + (tin + 1) * 128],
                                        ftv(xnT)[:, tin, c * 512:(c + 1) * 512],
                                        start=(tin == 0), stop=(tin == FT - 1)),
                               waits=w, inc=(tE if tin == FT - 1 else None))
                    last_mm = tE.v
                    dstT = QTs if which == "q" else KTs
                    scale = 0.125 if which == "q" else 1.0
                    bias = bq8S if which == "q" else bkS
                    er = P.emit("scalar", lambda s, dstT=dstT, pi=pi, c=c, j=j,
                                scale=scale, bias=bias, p=p:
                                s.activation(
                                    dstT[:, pi * SEQ + c * 512:pi * SEQ + (c + 1) * 512],
                                    qkP[j][:, 0:512], AF.Identity,
                                    bias=bias[:, p:p + 1], scale=scale),
                                waits=[(tE, last_mm), (dBk, ds_bk), (vE, ve_bq8)],
                                inc=aE)
                    last_ev = er
                    pool.release(j, aE, er)
                    yield
            wslot_gate[pi] = (tE, last_mm)
            red = 0
            for (kk, (which, srcT)) in enumerate((("q", QTs), ("k", KTs))):
                P.emit("vector", lambda v, srcT=srcT, pi=pi: v.tensor_mul(
                    qsqbf[:, 0:1024], srcT[:, pi * SEQ:pi * SEQ + 1024],
                    srcT[:, pi * SEQ:pi * SEQ + 1024]),
                    waits=[(aE, last_ev)])
                yield
                P.emit("vector", lambda v, srcT=srcT, pi=pi: v.tensor_mul(
                    qsqbf[:, 1024:2048],
                    srcT[:, pi * SEQ + 1024:pi * SEQ + 2048],
                    srcT[:, pi * SEQ + 1024:pi * SEQ + 2048]))
                sq = P.emit("vector", lambda v: v.drain(fusable=False), inc=vE)
                yield
                for c in range(NCH):
                    j, pw = pool.acquire()
                    mm = P.emit("tensor", lambda t, j=j, c=c: t.matmul(
                        qkP[j][0:2, 0:512], hselS[:, :],
                        qsqbf[:, c * 512:(c + 1) * 512], start=True, stop=True),
                        waits=[(vE, sq), (dHsel, ds_hsel)] + pw, inc=tE)
                    red = P.emit("vector", lambda v, j=j, cc=kk * 4 + c:
                                 v.reduce_max(cmS[:, cc:cc + 1],
                                              qkP[j][0:2, 0:512], AX.X),
                                 waits=[(tE, mm)], inc=vE)
                    pool.release(j, vE, red)
                    yield
            qm = P.emit("vector", lambda v: v.reduce_max(
                cmS[:, 8:9], cmS[:, 0:4], AX.X), waits=[(vE, red)], inc=vE)
            km = P.emit("vector", lambda v: v.reduce_max(
                cmS[:, 9:10], cmS[:, 4:8], AX.X), waits=[(vE, qm)], inc=vE)
            P.emit("vector", lambda v: v.tensor_mul(
                cmS[:, 10:11], cmS[:, 8:9], cmS[:, 9:10]), waits=[(vE, km)])
            pr_ = P.emit("vector", lambda v: v.drain(fusable=False), inc=vE)
            yield
            l1 = P.emit("scalar", lambda s: s.activation(
                cmS[:, 11:12], cmS[:, 10:11], AF.Ln), waits=[(vE, pr_)], inc=aE)
            P.emit("scalar", lambda s: s.activation(
                cmS[:, 11:12], cmS[:, 11:12], AF.Exp, scale=0.5),
                waits=[(aE, l1)])
            e1 = P.emit("scalar", lambda s: s.drain(fusable=False), inc=aE)
            yield
            P.emit("vector", lambda v: v.tensor_copy(cmbf[:, :], cmS[:, 11:12]),
                   waits=[(aE, e1)])
            cb = P.emit("vector", lambda v: v.drain(fusable=False), inc=vE)
            yield
            j, pw = pool.acquire()
            rm = P.emit("tensor", lambda t, j=j: t.matmul(
                qkP[j][:, 0:1], hselTS[:, :], cmbf[:, :], start=True, stop=True),
                waits=[(vE, cb)] + pw, inc=tE)
            ng = P.emit("scalar", lambda s, j=j, pi=pi: s.activation(
                negCS[:, pi:pi + 1], qkP[j][:, 0:1], AF.Copy, scale=-0.5),
                waits=[(tE, rm)], inc=aE)
            pool.release(j, aE, ng)
            negc[p] = ng
            if DEBUG_TAPS and p == PAIRS - 1:
                P.emit("gpsimd", lambda g: g.dma_start(
                    out=taps["t_qt"], in_=QTs[:, :]),
                    waits=[(aE, ng)], inc=dG, k=16)
                P.emit("gpsimd", lambda g: g.dma_start(
                    out=taps["t_kt"], in_=KTs[:, :]), inc=dG, k=16)
                P.emit("gpsimd", lambda g: g.dma_start(
                    out=taps["t_negc"], in_=negCS[:, :]), inc=dG, k=16)

        emit_qk_weights(0)
        emit_qk_weights(1)
        negc = [None] * PAIRS
        for _ in emit_qk_compute(0):
            pass

        v_done = 0
        with ExitStack() as WVS:
            WvS = WVS.enter_context(nc.sbuf_tensor("WvS", [128, FT * 512], BF16))
            for half in range(2):
                lastmm_for_dma = None
                if half == 1:
                    lastmm_for_dma = (tE, tE.v)
                P.emit("gpsimd", lambda g, half=half: g.dma_start(
                    out=WvS[:, :].rearrange("p (t f) -> p t f", t=FT),
                    in_=dwv.rearrange("(t p) f -> p t f", p=128)[:, :, half * 512:(half + 1) * 512]),
                    waits=([lastmm_for_dma] if lastmm_for_dma else []), inc=dWv, k=16)
                dg_wv = dWv.v
                for mt in range(NT):
                    j, pw = pool.acquire()
                    for tin in range(FT):
                        w = []
                        if tin == 0:
                            w = [(vE, xnT_all), (dWv, dg_wv)] + pw
                        P.emit("tensor", lambda t, j=j, mt=mt, half=half, tin=tin:
                               t.matmul(qkP[j][:, 0:512],
                                        ftv(xnT)[:, tin, mt * 128:(mt + 1) * 128],
                                        WvS[:, tin * 512:(tin + 1) * 512],
                                        start=(tin == 0), stop=(tin == FT - 1)),
                               waits=w, inc=(tE if tin == FT - 1 else None))
                    mmrec = tE.v
                    evw = [(tE, mmrec)]
                    if mt == 0 and half == 0:
                        evw.append((aE, ae_decay))
                    ev = P.emit("vector", lambda v, j=j, mt=mt, half=half:
                                v.tensor_copy(
                                    V[:, mt * 1024 + half * 512:mt * 1024 + (half + 1) * 512],
                                    qkP[j][:, 0:512]),
                                waits=evw, inc=vE)
                    v_done = ev
                    pool.release(j, vE, ev)
            P.emit("vector", lambda v: v.drain(fusable=False), inc=vE)
            v_done = vE.v
            v_mm_done = tE.v

        gscope = ExitStack()
        Wacc = gscope.enter_context(nc.sbuf_tensor("Wacc", [128, 2048], F32))
        g1b = gscope.enter_context(nc.sbuf_tensor("g1b", [128, 512], F32))
        g2b = gscope.enter_context(nc.sbuf_tensor("g2b", [128, 512], F32))

        wb_units = [(part, c, pt) for part in range(2) for c in range(4)
                    for pt in range(4)]
        wb_state = {"gfree": (tE, v_mm_done), "idx": 0, "done": None}

        def emit_wbuild_unit():
            if wb_state["idx"] >= len(wb_units):
                return
            part, c, pt = wb_units[wb_state["idx"]]
            wb_state["idx"] += 1
            gi = part * 4 + c
            wg = [] if wb_state["gfree"] is None else [wb_state["gfree"]]
            P.emit("sync", lambda s, gi=gi, pt=pt: s.dma_start(
                out=g1b[:, :], in_=dg1[gi, pt * 128:(pt + 1) * 128, :]),
                waits=wg, inc=dGrep, k=16)
            P.emit("sync", lambda s, gi=gi, pt=pt: s.dma_start(
                out=g2b[:, :], in_=dg2[gi, pt * 128:(pt + 1) * 128, :]),
                inc=dGrep, k=16)
            ds_g = dGrep.v
            if c == 0:
                u = P.emit("gpsimd", lambda g, pt=pt: g.tensor_mul(
                    Wacc[:, pt * 512:(pt + 1) * 512], g1b[:, :], g2b[:, :]),
                    waits=[(dGrep, ds_g)], inc=pE)
                wb_state["gfree"] = (pE, u)
            else:
                u1 = P.emit("gpsimd", lambda g: g.tensor_mul(
                    g1b[:, :], g1b[:, :], g2b[:, :]), waits=[(dGrep, ds_g)], inc=pE)
                u2 = P.emit("gpsimd", lambda g, pt=pt: g.tensor_add(
                    Wacc[:, pt * 512:(pt + 1) * 512],
                    Wacc[:, pt * 512:(pt + 1) * 512], g1b[:, :]),
                    waits=[(pE, u1)], inc=pE)
                wb_state["gfree"] = (pE, u2)
            if wb_state["idx"] == len(wb_units) // 2:
                last = 0
                for pt in range(4):
                    last = P.emit("gpsimd", lambda g, pt=pt: g.tensor_copy(
                        WfS[:, pt * D:pt * D + 512],
                        Wacc[:, pt * 512:(pt + 1) * 512]),
                        waits=([wb_state["gfree"]] if pt == 0 else []), inc=pE)
                    last = P.emit("gpsimd", lambda g, pt=pt: g.tensor_copy(
                        WfS[:, (4 + pt) * D + 512:(5 + pt) * D],
                        Wacc[:, pt * 512:(pt + 1) * 512]), inc=pE)
                wb_state["gfree"] = (pE, last)
            if wb_state["idx"] == len(wb_units):
                last = 0
                for pt in range(4):
                    last = P.emit("gpsimd", lambda g, pt=pt: g.tensor_copy(
                        WfS[:, pt * D + 512:(pt + 1) * D],
                        Wacc[:, pt * 512:(pt + 1) * 512]),
                        waits=([wb_state["gfree"]] if pt == 0 else []), inc=pE)
                    last = P.emit("gpsimd", lambda g, pt=pt: g.tensor_scalar_mul(
                        WfS[:, (4 + pt) * D:(4 + pt) * D + 512],
                        Wacc[:, pt * 512:(pt + 1) * 512], -1.0), inc=pE)
                last = P.emit("gpsimd", lambda g: g.drain(fusable=False), inc=pE)
                wb_state["done"] = last

        gstep = 0
        dbg_state = {}
        gav = {}
        exp_by_j = {0: None, 1: None}
        dg_wo = None
        with ExitStack() as AVS:
            pv = AVS.enter_context(nc.psum_tensor("pv0", [128, 512], F32))
            pr = AVS.enter_context(nc.psum_tensor("pr0", [128, 512], F32))
            aev_q = [None, None, None, None]
            pvpr_free = None
            aev_last = 0

            def gen_pair_filler(p):
                if p + 2 < PAIRS:
                    emit_qk_weights(p + 2)
                yield
                if p + 1 < PAIRS:
                    yield from emit_qk_compute(p + 1)
                for _ in range(5):
                    emit_wbuild_unit()
                    yield

            for p in range(PAIRS):
                pi = p % 2
                ng = negc[p]
                stp_wait = {0: ([(aE, exp_by_j[0])] if exp_by_j[0] else []),
                            1: ([(aE, exp_by_j[1])] if exp_by_j[1] else [])}
                steps = [(c, mt) for c in range(NCH) for mt in range(4 * c + 4)]
                S = len(steps)
                g0 = gstep
                strec = {}
                er = {}
                erh = {}
                mrr = {}

                def emit_ST(k):
                    c, mt = steps[k]
                    j = k % 2
                    no = max(0, 128 * mt - 512 * c)
                    if k < 2:
                        w = [(aE, ng)] + stp_wait[j]
                    elif last_av[0] < k - 2:
                        w = [(aE, er[k - 2])]
                    else:
                        w = []
                    P.emit("tensor", lambda t, j=j, pi=pi, mt=mt, c=c, no=no: t.matmul(
                        stP[j][:, no:512],
                        KTs[0:64, pi * SEQ + mt * 128:pi * SEQ + (mt + 1) * 128],
                        QTs[0:64, pi * SEQ + c * 512 + no:pi * SEQ + (c + 1) * 512],
                        start=True, stop=True, tile_position=(0, 0)),
                        waits=w)
                    P.emit("tensor", lambda t, j=j, pi=pi, mt=mt, c=c, no=no: t.matmul(
                        stP[j][:, 512 + no:1024],
                        KTs[64:128, pi * SEQ + mt * 128:pi * SEQ + (mt + 1) * 128],
                        QTs[64:128, pi * SEQ + c * 512 + no:pi * SEQ + (c + 1) * 512],
                        start=True, stop=True, tile_position=(64, 0)),
                        inc=(None if PARANOID else tE))
                    if PARANOID:
                        P.emit("tensor", lambda t: t.drain(fusable=False), inc=tE)
                    strec[k] = tE.v

                def emit_exp(k):
                    c, mt = steps[k]
                    j = k % 2
                    no = max(0, 128 * mt - 512 * c)
                    slot = (g0 + k) % 4
                    we = [(tE, strec[k])]
                    if (g0 + k - 4) in gav:
                        we.append((tE, gav[g0 + k - 4]))
                    if no == 0:
                        e_ = P.emit("scalar", lambda s, j=j, slot=slot, pi=pi:
                                    s.activation(
                                        PTb[:, slot * 1024:(slot + 1) * 1024],
                                        stP[j][:, 0:1024], AF.Exp,
                                        bias=negCS[:, pi:pi + 1]),
                                    waits=we, inc=aE)
                        eh = e_
                    else:
                        eh = P.emit("scalar", lambda s, j=j, slot=slot, pi=pi, no=no:
                                    s.activation(PTb[:, slot * 1024 + no:slot * 1024 + 512],
                                                 stP[j][:, no:512], AF.Exp,
                                                 bias=negCS[:, pi:pi + 1]),
                                    waits=we, inc=aE)
                        e_ = P.emit("scalar", lambda s, j=j, slot=slot, pi=pi, no=no:
                                    s.activation(PTb[:, slot * 1024 + 512 + no:(slot + 1) * 1024],
                                                 stP[j][:, 512 + no:1024], AF.Exp,
                                                 bias=negCS[:, pi:pi + 1]),
                                    inc=aE)
                    er[k] = e_
                    erh[k] = eh
                    stp_wait[j] = [(aE, e_)]
                    exp_by_j[j] = e_

                def emit_mult(k):
                    c, mt = steps[k]
                    slot = (g0 + k) % 4
                    no = max(0, 128 * mt - 512 * c)
                    off = 384 + c * 512 - mt * 128
                    P.emit("vector", lambda v, slot=slot, off=off, no=no: v.tensor_mul(
                        PTb[:, slot * 1024 + no:slot * 1024 + 512],
                        PTb[:, slot * 1024 + no:slot * 1024 + 512],
                        decayS[:, off + no:off + 512]),
                        waits=[(aE, erh[k]), (aE, ae_decay)])
                    m_ = P.emit("vector", lambda v, slot=slot, off=off, no=no: v.tensor_mul(
                        PTb[:, slot * 1024 + 512 + no:(slot + 1) * 1024],
                        PTb[:, slot * 1024 + 512 + no:(slot + 1) * 1024],
                        decayS[:, off + no:off + 512]),
                        waits=[(aE, er[k])], inc=vE)
                    mrr[k] = m_

                def emit_attnV(k, pgate=None):
                    nonlocal aev_last, pvpr_free
                    c, mt = steps[k]
                    nmt = 4 * c + 4
                    slot = (g0 + k) % 4
                    no = max(0, 128 * mt - 512 * c)
                    wv_ = [(vE, pgate if pgate is not None else mrr[k])]
                    if mt == 0:
                        wv_.append((vE, v_done))
                        wv_.append((vE, ve_ones))
                        if pvpr_free is not None:
                            wv_.append(pvpr_free)
                    h0, h1 = 2 * p, 2 * p + 1
                    islast = (mt == nmt - 1)
                    P.emit("tensor", lambda t, mt=mt, h0=h0, slot=slot,
                           islast=islast, no=no:
                           t.matmul(pv[0:64, no:512],
                                    V[:, mt * 1024 + h0 * 64:mt * 1024 + h0 * 64 + 64],
                                    PTb[:, slot * 1024 + no:slot * 1024 + 512],
                                    start=(mt == 0), stop=islast,
                                    tile_position=(0, 0), skip_group_check=True),
                           waits=wv_)
                    P.emit("tensor", lambda t, mt=mt, h1=h1, slot=slot,
                           islast=islast, no=no:
                           t.matmul(pv[64:128, no:512],
                                    V[:, mt * 1024 + h1 * 64:mt * 1024 + h1 * 64 + 64],
                                    PTb[:, slot * 1024 + 512 + no:(slot + 1) * 1024],
                                    start=(mt == 0), stop=islast,
                                    tile_position=(0, 64), skip_group_check=True))
                    P.emit("tensor", lambda t, mt=mt, slot=slot,
                           islast=islast, no=no:
                           t.matmul(pr[0:64, no:512], ones64[:, :],
                                    PTb[:, slot * 1024 + no:slot * 1024 + 512],
                                    start=(mt == 0), stop=islast,
                                    tile_position=(0, 0), skip_group_check=True))
                    P.emit("tensor", lambda t, mt=mt, slot=slot,
                           islast=islast, no=no:
                           t.matmul(pr[64:128, no:512], ones64[:, :],
                                    PTb[:, slot * 1024 + 512 + no:(slot + 1) * 1024],
                                    start=(mt == 0), stop=islast,
                                    tile_position=(0, 64), skip_group_check=True),
                           inc=(None if (PARANOID and islast) else tE))
                    if PARANOID and islast:
                        P.emit("tensor", lambda t: t.drain(fusable=False), inc=tE)
                    gav[g0 + k] = tE.v
                    if (QUARTER_NORM and mt >= 4 * c) or \
                       (not QUARTER_NORM and islast):
                        if QUARTER_NORM:
                            q = mt - 4 * c
                            lo, wd = 128 * q, 128
                        else:
                            q = 3
                            lo, wd = 0, 512
                        h = wd // 2
                        rcw = [(tE, gav[g0 + k])]
                        if aev_q[q] is not None:
                            rcw.append(aev_q[q])
                        rl = P.emit("scalar", lambda s, lo=lo, wd=wd: s.activation(
                            recipS[:, lo:lo + wd], pr[:, lo:lo + wd], AF.Ln),
                            waits=rcw, inc=aE)
                        ra = P.emit("scalar", lambda s, lo=lo, h=h: s.activation(
                            recipS[:, lo:lo + h], recipS[:, lo:lo + h],
                            AF.Exp, scale=-1.0), waits=[(aE, rl)], inc=aE)
                        rb = P.emit("scalar", lambda s, lo=lo, h=h: s.activation(
                            recipS[:, lo + h:lo + 2 * h],
                            recipS[:, lo + h:lo + 2 * h],
                            AF.Exp, scale=-1.0), inc=aE)
                        base = p * SEQ + c * 512 + lo
                        P.emit("vector", lambda v, lo=lo, base=base, h=h: v.tensor_mul(
                            attn_outT[:, base:base + h],
                            pv[:, lo:lo + h], recipS[:, lo:lo + h]),
                            waits=[(aE, ra)])
                        ae_ = P.emit("vector", lambda v, lo=lo, base=base, h=h: v.tensor_mul(
                            attn_outT[:, base + h:base + 2 * h],
                            pv[:, lo + h:lo + 2 * h], recipS[:, lo + h:lo + 2 * h]),
                            waits=[(aE, rb)], inc=vE)
                        aev_q[q] = (vE, ae_)
                        aev_last = ae_
                        if q == 3:
                            pvpr_free = (vE, ae_)
                    if islast and DEBUG_TAPS and p == PAIRS - 1 and c == 0:
                        d1 = P.emit("vector", lambda v: v.tensor_copy(
                            sqscr[:, 0:512], pv[:, :]),
                            waits=[(tE, gav[g0 + k])], inc=vE)
                        d2 = P.emit("vector", lambda v: v.tensor_copy(
                            sqscr[:, 512:1024], pr[:, :]),
                            waits=[(vE, d1)], inc=vE)
                        pvpr_free = (vE, d2)
                        P.emit("sync", lambda s: s.dma_start(
                            out=taps["t_pvpr"], in_=sqscr[:, :]),
                            waits=[(vE, d2)], inc=dS, k=16)
                        P.emit("gpsimd", lambda g: g.dma_start(
                            out=taps["t_v"], in_=V[:, :]), inc=dG, k=16)

                av_sched = {}
                prev_it = -1
                for kk in range(S):
                    mt0_ = steps[kk][1]
                    des = kk + (4 if mt0_ == 0 else 2)
                    it = max(des, prev_it)
                    prev_it = it
                    av_sched.setdefault(it, []).append(kk)
                last_av = [-1]
                filler = gen_pair_filler(p)
                for k in range(S):
                    if INTERLEAVE_QK:
                        next(filler, None)
                    for kk in av_sched.get(k, []):
                        emit_attnV(kk)
                        last_av[0] = kk
                    emit_ST(k)
                    emit_exp(k)
                    emit_mult(k)
                for it in range(S, prev_it + 1):
                    for kk in av_sched.get(it, []):
                        emit_attnV(kk)
                        last_av[0] = kk
                for _ in filler:
                    pass
                gstep = g0 + S
                qkslot_gate[pi] = (tE, gav[gstep - 1])
            attn_done_t = tE.v
            P.emit("vector", lambda v: v.drain(fusable=False),
                   waits=[(vE, aev_last)], inc=vE)
            attn_done_v = vE.v

        pscope.close()
        gscope.close()
        attnscope.close()

        WoS = TOP.enter_context(nc.sbuf_tensor("WoS", [128, FT * D], BF16))
        P.emit("gpsimd", lambda g: g.dma_start(
            out=WoS[:, :].rearrange("p (t f) -> p t f", t=FT),
            in_=dwo.rearrange("(t p) f -> p t f", p=128)),
            waits=[(tE, attn_done_t)], inc=dWo, k=16)
        dg_wo = dWo.v
        boS = TOP.enter_context(nc.sbuf_tensor("boS", [1, D], F32))
        bfHi = TOP.enter_context(nc.sbuf_tensor("bfHi", [1, D], BF16))
        bfLo = TOP.enter_context(nc.sbuf_tensor("bfLo", [1, D], BF16))
        with nc.sbuf_tensor("bfS", [1, D], F32) as bfS:
            P.emit("sync", lambda s: s.dma_start(out=bfS[:, :], in_=dbf),
                   waits=[(tE, attn_done_t)], inc=dBf, k=16)
            ds_bf = dBf.v
            P.emit("vector", lambda v, bfS=bfS: v.tensor_copy(bfHi[:, :], bfS[:, :]),
                   waits=[(dBf, ds_bf), (vE, attn_done_v)], inc=vE)
            ve_bfhi = vE.v
            P.emit("vector", lambda v, bfS=bfS: v.tensor_sub(
                bfLo[:, :], bfS[:, :], bfHi[:, :]), waits=[(vE, ve_bfhi)])
            ve_bflo = P.emit("vector", lambda v: v.drain(fusable=False), inc=vE)
        P.emit("sync", lambda s: s.dma_start(out=boS[:, :], in_=dbo),
               waits=[(tE, attn_done_t)], inc=dBo, k=16)
        ds_bo = dBo.v
        x2 = TOP.enter_context(nc.sbuf_tensor("x2", [128, NT * 1024], F32))
        outbuf = TOP.enter_context(nc.sbuf_tensor("outbuf", [128, 2 * 1024], F32))
        wf_done = wb_state["done"]

        with ExitStack() as PSB:
            with ExitStack() as BOS:
                psBO = BOS.enter_context(nc.psum_tensor("psBO", [1, 1024], F32))
                for ch in range(2):
                    for tin in range(FT):
                        w = []
                        if tin == 0 and ch == 0:
                            w = [(dWo, dg_wo), (dBv, dg_bv), (tE, attn_done_t)]
                        P.emit("tensor", lambda t, ch=ch, tin=tin: t.matmul(
                            psBO[0:1, ch * 512:(ch + 1) * 512], bvS[:, tin:tin + 1],
                            WoS[:, tin * D + ch * 512:tin * D + (ch + 1) * 512],
                            start=(tin == 0), stop=(tin == FT - 1)),
                            waits=w,
                            inc=(tE if (tin == FT - 1 and ch == 1) else None))
                bo_mm = tE.v
                b1_ = P.emit("vector", lambda v: v.tensor_add(
                    boS[:, :], psBO[0:1, :], boS[:, :]),
                    waits=[(tE, bo_mm), (dBo, ds_bo), (pE, wf_done),
                           (vE, attn_done_v)], inc=vE)
                b2_ = P.emit("vector", lambda v: v.tensor_copy(
                    boHi[:, :], boS[:, :]), waits=[(vE, b1_)], inc=vE)
                P.emit("vector", lambda v: v.tensor_sub(
                    boLo[:, :], boS[:, :], boHi[:, :]), waits=[(vE, b2_)])
                bo_ready = P.emit("vector", lambda v: v.drain(fusable=False), inc=vE)

            psW = [PSB.enter_context(nc.psum_tensor("psW0", [128, 512], F32)),
                   PSB.enter_context(nc.psum_tensor("psW1", [128, 512], F32))]
            psT2 = [PSB.enter_context(nc.psum_tensor("psT2a", [128, 1024], BF16)),
                    PSB.enter_context(nc.psum_tensor("psT2b", [128, 1024], BF16))]
            psF = [PSB.enter_context(nc.psum_tensor("psF0", [128, 1024], F32)),
                   PSB.enter_context(nc.psum_tensor("psF1", [128, 1024], F32))]
            wpool = PsumPool(psW)
            psT2_free = [None, None]
            psF_free = [None, None]
            resid1 = [0] * NT
            x2e_rec = [0] * NT
            out_dma = {}
            dSg = Sem(TOP, nc, "dSg")
            ffn_ev = [0] * NT

            def stageA(i):
                sl = (i % 2) * 1024
                wslot = [xslot_free[i % 2]] if xslot_free[i % 2] is not None else []
                P.emit("sync", lambda s, i=i, sl=sl: s.dma_start(
                    out=xrot[:, sl:sl + 1024], in_=dx[i * 128:(i + 1) * 128, :]),
                    waits=wslot, inc=dX[i % 2], k=16)
                ds_x2 = (dX[i % 2], dX[i % 2].v)
                for ch in range(2):
                    j, pw = wpool.acquire()
                    for tin in range(FT):
                        w = []
                        if tin == 0:
                            w = [(tE, attn_done_t), (vE, attn_done_v),
                                 (dWo, dg_wo), (vE, bo_ready)] + pw
                        P.emit("tensor", lambda t, j=j, i=i, ch=ch, tin=tin:
                               t.matmul(psW[j][:, :],
                                        attn_outT[:, tin * SEQ + i * 128:tin * SEQ + (i + 1) * 128],
                                        WoS[:, tin * D + ch * 512:tin * D + (ch + 1) * 512],
                                        start=(tin == 0), stop=False),
                               waits=w)
                    P.emit("tensor", lambda t, j=j, ch=ch: t.matmul(
                        psW[j][:, :], ones1[:, :],
                        boHi[0:1, ch * 512:(ch + 1) * 512],
                        start=False, stop=False))
                    P.emit("tensor", lambda t, j=j, ch=ch: t.matmul(
                        psW[j][:, :], ones1[:, :],
                        boLo[0:1, ch * 512:(ch + 1) * 512],
                        start=False, stop=True), inc=tE)
                    mmrec = tE.v
                    rv = P.emit("vector", lambda v, j=j, i=i, ch=ch, sl=sl:
                                v.tensor_add(
                                    x2[:, i * 1024 + ch * 512:i * 1024 + (ch + 1) * 512],
                                    psW[j][:, :],
                                    xrot[:, sl + ch * 512:sl + (ch + 1) * 512]),
                                waits=[(tE, mmrec), ds_x2], inc=vE)
                    wpool.release(j, vE, rv)
                    resid1[i] = rv
                xslot_free[i % 2] = (vE, resid1[i])

            ln2st = {}
            ln2rs = {}

            def stageB_stats(i):
                ln2st[i] = emit_ln_stats(
                    i, x2[:, i * 1024:(i + 1) * 1024], [(vE, resid1[i])])

            def stageB_ln(i):
                ln2rs[i] = emit_ln_lnact(i, ln2st[i])

            def stageB(i):
                sl = (i % 2) * 1024
                extra = [] if xnrot_tr_gate[i % 2] is None else [xnrot_tr_gate[i % 2]]
                ap = emit_ln_apply(i, x2[:, i * 1024:(i + 1) * 1024], ln2rs[i], extra)
                tw = [(vE, ap)]
                if psT2_free[i % 2] is not None:
                    tw.append(psT2_free[i % 2])
                for j in range(FT):
                    P.emit("tensor", lambda t, i=i, j=j, sl=sl: t.transpose(
                        psT2[i % 2][:, j * 128:(j + 1) * 128],
                        xnrot[:, sl + j * 128:sl + (j + 1) * 128], identS[:, :]),
                        waits=(tw if j == 0 else []),
                        inc=(tE if j == FT - 1 else None))
                trrec = tE.v
                xnrot_tr_gate[i % 2] = (tE, trrec)
                dst = ftv(xnT)[:, :, i * 128:(i + 1) * 128]
                srcT = psT2[i % 2][:, :].rearrange("p (a n) -> p a n", a=FT)
                xe = P.emit("vector", lambda v, dst=dst, srcT=srcT: v.tensor_copy(
                    dst, srcT), waits=[(tE, trrec)], inc=vE)
                x2e_rec[i] = xe
                psT2_free[i % 2] = (vE, xe)

            def stageC(i):
                j = i % 2
                for ch in range(2):
                    for tin in range(FT):
                        w = []
                        if tin == 0:
                            if ch == 0:
                                w = [(vE, x2e_rec[i]), (pE, wf_done)]
                                if psF_free[j] is not None:
                                    w.append(psF_free[j])
                        P.emit("tensor", lambda t, j=j, i=i, ch=ch, tin=tin:
                               t.matmul(psF[j][:, ch * 512:(ch + 1) * 512],
                                        ftv(xnT)[:, tin, i * 128:(i + 1) * 128],
                                        WfS[:, tin * D + ch * 512:tin * D + (ch + 1) * 512],
                                        start=(tin == 0), stop=False),
                               waits=w)
                    P.emit("tensor", lambda t, j=j, ch=ch: t.matmul(
                        psF[j][:, ch * 512:(ch + 1) * 512], ones1[:, :],
                        bfHi[0:1, ch * 512:(ch + 1) * 512],
                        start=False, stop=False),
                        waits=([(vE, ve_bflo)] if (i == 0 and ch == 0) else []))
                    P.emit("tensor", lambda t, j=j, ch=ch: t.matmul(
                        psF[j][:, ch * 512:(ch + 1) * 512], ones1[:, :],
                        bfLo[0:1, ch * 512:(ch + 1) * 512],
                        start=False, stop=True),
                        inc=(tE if ch == 1 else None))
                mmrec = tE.v
                wev = [(tE, mmrec)]
                if i >= 2 and (i // 2 - 1) in out_dma:
                    wev.append(out_dma[i // 2 - 1])
                ev = P.emit("vector", lambda v, j=j, i=i: v.tensor_add(
                    outbuf[:, (i % 2) * 1024:(i % 2 + 1) * 1024], psF[j][:, :],
                    x2[:, i * 1024:(i + 1) * 1024]),
                    waits=wev, inc=vE)
                ffn_ev[i] = ev
                psF_free[j] = (vE, ev)
                if i % 2 == 1:
                    g = i // 2
                    dst = dout[g * 256:(g + 1) * 256, :].rearrange(
                        "(t p) f -> p t f", p=128)
                    srcb = outbuf[:, :].rearrange("p (t f) -> p t f", t=2)
                    wdr = P.emit("vector", lambda v: v.drain(fusable=False),
                                 waits=[(vE, ffn_ev[i])], inc=vE)
                    qeng, qsem = (("sync", dS) if g % 2 == 0
                                  else ("gpsimd", dSg))
                    P.emit(qeng, lambda s, dst=dst, srcb=srcb: s.dma_start(
                        out=dst, in_=srcb), waits=[(vE, wdr)], inc=qsem, k=16)
                    out_dma[g] = (qsem, qsem.v)

            for i in range(NT + 4):
                if 4 <= i and i - 4 < NT:
                    stageC(i - 4)
                if 3 <= i and i - 3 < NT:
                    stageB(i - 3)
                if 2 <= i and i - 2 < NT:
                    stageB_ln(i - 2)
                if 1 <= i and i - 1 < NT:
                    stageB_stats(i - 1)
                if i < NT:
                    stageA(i)

            P.emit("sync", lambda s: s.nop(),
                   waits=[out_dma[NT // 2 - 2], out_dma[NT // 2 - 1]])
            if DEBUG_TAPS:
                P.emit("gpsimd", lambda g: g.dma_start(out=taps["t_attn"], in_=attn_outT[:, :]),
                       waits=[out_dma[NT // 2 - 2], out_dma[NT // 2 - 1]], inc=dG, k=16)
                P.emit("gpsimd", lambda g: g.dma_start(out=taps["t_xnT"], in_=xnT[:, :]), inc=dG, k=16)
                P.emit("gpsimd", lambda g: g.dma_start(out=taps["t_wf"], in_=WfS[:, :]), inc=dG, k=16)
                P.emit("gpsimd", lambda g: g.dma_start(out=taps["t_decay"], in_=decayS[:, :]), inc=dG, k=16)
                P.emit("sync", lambda s: s.dma_start(out=taps["t_x2"], in_=x2[:, :]), inc=dS, k=16)
                fin = dS.v
                P.emit("gpsimd", lambda g: g.nop(), waits=[(dG, dG.v), (dS, fin)])

        blk = TOP.enter_context(nc.Block())

        @blk.sync
        def _(e):
            for op in P.ops["sync"]:
                op(e)

        @blk.gpsimd
        def _(e):
            for op in P.ops["gpsimd"]:
                op(e)

        @blk.vector
        def _(e):
            for op in P.ops["vector"]:
                op(e)

        @blk.scalar
        def _(e):
            for op in P.ops["scalar"]:
                op(e)

        @blk.tensor
        def _(e):
            for op in P.ops["tensor"]:
                op(e)

    return nc


_NC_CACHE = {}


def _get_program():
    if "nc" not in _NC_CACHE:
        _NC_CACHE["nc"] = build_program()
    return _NC_CACHE["nc"]


def kernel(L0_attn_x, L0_attn_Wq, L0_attn_bq, L0_attn_Wk, L0_attn_bk,
           L0_attn_Wv, L0_attn_bv, L0_attn_Wo, L0_attn_bo,
           L0_ffn_G1_R, L0_ffn_G2_R, L0_ffn_G1_I, L0_ffn_G2_I,
           L0_ffn_bias_real, L0_ffn_bias_imag):
    x = np.ascontiguousarray(np.asarray(L0_attn_x, dtype=np.float32))
    wq = np.ascontiguousarray(np.asarray(L0_attn_Wq, dtype=np.float32))
    wk = np.ascontiguousarray(np.asarray(L0_attn_Wk, dtype=np.float32))
    wv = np.ascontiguousarray(np.asarray(L0_attn_Wv, dtype=np.float32))
    wo = np.ascontiguousarray(np.asarray(L0_attn_Wo, dtype=np.float32))
    bq = np.ascontiguousarray(np.asarray(L0_attn_bq, dtype=np.float32))
    bk = np.ascontiguousarray(np.asarray(L0_attn_bk, dtype=np.float32))
    bv = np.ascontiguousarray(np.asarray(L0_attn_bv, dtype=np.float32))
    bo = np.ascontiguousarray(np.asarray(L0_attn_bo, dtype=np.float32)).reshape(1, D)
    bfn = np.ascontiguousarray(np.concatenate([
        np.asarray(L0_ffn_bias_real, dtype=np.float32),
        np.asarray(L0_ffn_bias_imag, dtype=np.float32)]).reshape(1, D))

    def reps(G1, G2):
        g1l, g2l = [], []
        for c in range(4):
            X = np.asarray(G1, dtype=np.float32)[0, :, c, :]
            Y = np.asarray(G2, dtype=np.float32)[c, :, 0, :]
            g1l.append(np.ascontiguousarray(np.broadcast_to(
                X.T[:, None, :, None], (32, 16, 32, 16)).reshape(512, 512)))
            g2l.append(np.ascontiguousarray(np.broadcast_to(
                Y.T[None, :, None, :], (32, 16, 32, 16)).reshape(512, 512)))
        return g1l, g2l

    g1R, g2R = reps(L0_ffn_G1_R, L0_ffn_G2_R)
    g1I, g2I = reps(L0_ffn_G1_I, L0_ffn_G2_I)
    g1rep = np.ascontiguousarray(np.stack(g1R + g1I))
    g2rep = np.ascontiguousarray(np.stack(g2R + g2I))

    ident = np.eye(128, dtype=ml_dtypes.bfloat16)
    dgrid = (np.arange(DEC_W, dtype=np.float32)[None, :] - 384.0
             - np.arange(128, dtype=np.float32)[:, None]).astype(np.float32)
    hsel = np.zeros((128, 2), dtype=ml_dtypes.bfloat16)
    hsel[0:64, 0] = 1
    hsel[64:128, 1] = 1
    hselT = np.ones((2, 128), dtype=ml_dtypes.bfloat16)

    nc = _get_program()
    shared = {"wq": wq, "wk": wk, "wv": wv, "wo": wo, "bq": bq, "bk": bk,
              "bv": bv, "bo": bo, "bfn": bfn, "g1rep": g1rep, "g2rep": g2rep,
              "ident": ident, "dgrid": dgrid, "hsel": hsel, "hselT": hselT}
    in_maps = [dict(shared, x=np.ascontiguousarray(x[b])) for b in range(8)]
    res = run_bass_kernel_spmd(nc, in_maps, core_ids=list(range(8)))
    out = np.stack([np.asarray(r["out"], dtype=np.float32) for r in res.results])
    return out


# version 26
# speedup vs baseline: 1.1906x; 1.1906x over previous
import numpy as np
import ml_dtypes
from contextlib import ExitStack

import concourse.bass as bass
import concourse.mybir as mybir
from concourse.bass_utils import run_bass_kernel_spmd

F32 = mybir.dt.float32
BF16 = mybir.dt.bfloat16
AF = mybir.ActivationFunctionType
ALU = mybir.AluOpType
AX = mybir.AxisListType

SEQ, D, H, DK = 2048, 1024, 16, 64
NT = SEQ // 128
FT = D // 128
PAIRS = FT
NCH = SEQ // 512
DEC_W = 2432


class Sem:
    def __init__(self, stack, nc, name):
        self.h = stack.enter_context(nc.semaphore(name))
        self.v = 0


class Plan:
    def __init__(self):
        self.ops = {k: [] for k in ("sync", "gpsimd", "vector", "scalar", "tensor")}

    def emit(self, eng, build_fn, waits=(), inc=None, k=1):
        w = tuple((s.h, int(v)) for (s, v) in waits if s is not None)
        if inc is not None:
            inc.v += k
        def op(e, w=w, build_fn=build_fn, inc=inc, k=k):
            for (h, val) in w:
                e.wait_ge(h, val)
            ins = build_fn(e)
            if inc is not None:
                ins.then_inc(inc.h, k)
        self.ops[eng].append(op)
        return inc.v if inc is not None else None


class PsumPool:
    def __init__(self, tensors):
        self.t = tensors
        self.free_req = [None] * len(tensors)
        self.i = 0

    def acquire(self):
        j = self.i % len(self.t)
        self.i += 1
        req = self.free_req[j]
        return j, ([] if req is None else [req])

    def release(self, j, sem, val):
        self.free_req[j] = (sem, val)


DEBUG_TAPS = False
PARANOID = False
INTERLEAVE_QK = True
QUARTER_NORM = False


def build_program():
    nc = bass.Bass("TRN2", target_bir_lowering=False)

    dx = nc.dram_tensor("x", [SEQ, D], F32, kind="ExternalInput").ap()
    dwq = nc.dram_tensor("wq", [D, D], F32, kind="ExternalInput").ap()
    dwk = nc.dram_tensor("wk", [D, D], F32, kind="ExternalInput").ap()
    dwv = nc.dram_tensor("wv", [D, D], F32, kind="ExternalInput").ap()
    dwo = nc.dram_tensor("wo", [D, D], F32, kind="ExternalInput").ap()
    dbq = nc.dram_tensor("bq", [D], F32, kind="ExternalInput").ap()
    dbk = nc.dram_tensor("bk", [D], F32, kind="ExternalInput").ap()
    dbv = nc.dram_tensor("bv", [D], F32, kind="ExternalInput").ap()
    dbo = nc.dram_tensor("bo", [1, D], F32, kind="ExternalInput").ap()
    dbf = nc.dram_tensor("bfn", [1, D], F32, kind="ExternalInput").ap()
    dg1 = nc.dram_tensor("g1rep", [8, 512, 512], F32, kind="ExternalInput").ap()
    dg2 = nc.dram_tensor("g2rep", [8, 512, 512], F32, kind="ExternalInput").ap()
    dident = nc.dram_tensor("ident", [128, 128], BF16, kind="ExternalInput").ap()
    dgrid = nc.dram_tensor("dgrid", [128, DEC_W], F32, kind="ExternalInput").ap()
    dhsel = nc.dram_tensor("hsel", [128, 2], BF16, kind="ExternalInput").ap()
    dhselT = nc.dram_tensor("hselT", [2, 128], BF16, kind="ExternalInput").ap()
    dout = nc.dram_tensor("out", [SEQ, D], F32, kind="ExternalOutput").ap()
    taps = {}
    if DEBUG_TAPS:
        for nm, shp, dt in [("t_attn", [128, FT * SEQ], F32), ("t_x2", [128, NT * 1024], F32),
                            ("t_xnT", [128, FT * SEQ], F32), ("t_wf", [128, FT * D], F32),
                            ("t_decay", [128, DEC_W], F32), ("t_qt", [128, 2 * SEQ], F32),
                            ("t_kt", [128, 2 * SEQ], F32), ("t_negc", [128, 2], F32),
                            ("t_pt", [128, 2 * 1024], F32), ("t_pvpr", [128, 1024], F32),
                            ("t_v", [128, NT * 1024], F32)]:
            taps[nm] = nc.dram_tensor(nm, shp, dt, kind="ExternalOutput").ap()

    P = Plan()

    with ExitStack() as TOP:
        E = TOP.enter_context
        dS = Sem(TOP, nc, "dS")
        dG = Sem(TOP, nc, "dG")
        dG2 = Sem(TOP, nc, "dG2")
        dGrid = Sem(TOP, nc, "dGrid")
        dIdent = Sem(TOP, nc, "dIdent")
        dHsel = Sem(TOP, nc, "dHsel")
        dBq = Sem(TOP, nc, "dBq")
        dBk = Sem(TOP, nc, "dBk")
        dBo = Sem(TOP, nc, "dBo")
        dBf = Sem(TOP, nc, "dBf")
        dBv = Sem(TOP, nc, "dBv")
        dWv = Sem(TOP, nc, "dWv")
        dWo = Sem(TOP, nc, "dWo")
        dGrep = Sem(TOP, nc, "dGrep")
        dX = [Sem(TOP, nc, "dX0"), Sem(TOP, nc, "dX1"), Sem(TOP, nc, "dX2")]
        dXg = [Sem(TOP, nc, "dXg0"), Sem(TOP, nc, "dXg1"), Sem(TOP, nc, "dXg2")]
        tE = Sem(TOP, nc, "tE")
        vE = Sem(TOP, nc, "vE")
        aE = Sem(TOP, nc, "aE")
        pE = Sem(TOP, nc, "pE")

        attn_outT = E(nc.sbuf_tensor("attn_outT", [128, FT * SEQ], BF16))
        xnT = E(nc.sbuf_tensor("xnT", [128, FT * SEQ], BF16))
        xrot = E(nc.sbuf_tensor("xrot", [128, 3 * 1024], F32))
        xnrot = E(nc.sbuf_tensor("xnrot", [128, 2 * 1024], BF16))
        sqscr = E(nc.sbuf_tensor("sqscr", [128, 1024], F32))
        stat = E(nc.sbuf_tensor("stat", [128, 96], F32))
        identS = E(nc.sbuf_tensor("identS", [128, 128], BF16))
        ones1 = E(nc.sbuf_tensor("ones1", [1, 128], BF16))
        ones64 = E(nc.sbuf_tensor("ones64", [128, 64], BF16))
        hselS = E(nc.sbuf_tensor("hselS", [128, 2], BF16))
        hselTS = E(nc.sbuf_tensor("hselTS", [2, 128], BF16))
        bqS = E(nc.sbuf_tensor("bqS", [128, 8], F32))
        bq8S = E(nc.sbuf_tensor("bq8S", [128, 8], F32))
        bkS = E(nc.sbuf_tensor("bkS", [128, 8], F32))
        bvS = E(nc.sbuf_tensor("bvS", [128, 8], BF16))
        boHi = E(nc.sbuf_tensor("boHi", [1, D], BF16))
        boLo = E(nc.sbuf_tensor("boLo", [1, D], BF16))
        decayS = E(nc.sbuf_tensor("decayS", [128, DEC_W], BF16))

        def ftv(t):
            return t[:, :].rearrange("p (a n) -> p a n", a=FT)

        P.emit("gpsimd", lambda g: g.dma_start(
            out=bvS[:, :], in_=dbv.rearrange("(t p) -> p t", p=128),
            allow_slow_non_contiguous=True), inc=dBv, k=16)
        dg_bv = dBv.v

        maskscope = ExitStack()
        B1 = maskscope.enter_context(nc.sbuf_tensor("B1", [128, DEC_W], F32))
        B2 = maskscope.enter_context(nc.sbuf_tensor("B2", [128, DEC_W], F32))
        P.emit("sync", lambda s: s.dma_start(out=B1[:, :], in_=dgrid), inc=dGrid, k=16)
        ds_grid = dGrid.v
        P.emit("sync", lambda s: s.dma_start(out=identS[:, :], in_=dident), inc=dIdent, k=16)
        ds_ident = dIdent.v
        P.emit("sync", lambda s: s.dma_start(out=hselS[:, :], in_=dhsel), inc=dHsel, k=16)
        P.emit("sync", lambda s: s.dma_start(out=hselTS[:, :], in_=dhselT), inc=dHsel, k=16)
        ds_hsel = dHsel.v
        P.emit("sync", lambda s: s.dma_start(
            out=bqS[:, :], in_=dbq.rearrange("(t p) -> p t", p=128),
            allow_slow_non_contiguous=True), inc=dBq, k=16)
        ds_bq = dBq.v
        P.emit("sync", lambda s: s.dma_start(
            out=bkS[:, :], in_=dbk.rearrange("(t p) -> p t", p=128),
            allow_slow_non_contiguous=True), inc=dBk, k=16)
        ds_bk = dBk.v

        P.emit("vector", lambda v: v.memset(ones1[:, :], 1.0))
        P.emit("vector", lambda v: v.memset(ones64[:, :], 1.0))
        ve_ones = P.emit("vector", lambda v: v.drain(fusable=False), inc=vE)

        mask_state = {}

        def mask_step(step):
            if step == 0:
                m0 = P.emit("vector", lambda v: v.tensor_scalar(
                    B2[:, :], B1[:, :], 0.0, 1e30, ALU.min, ALU.mult),
                    waits=[(dGrid, ds_grid)], inc=vE)
                P.emit("vector", lambda v: v.tensor_scalar_max(
                    B1[:, :], B1[:, :], 1.0), waits=[(vE, m0)])
                mask_state["ve_b1"] = P.emit(
                    "vector", lambda v: v.drain(fusable=False), inc=vE)
            elif step == 1:
                P.emit("scalar", lambda s: s.activation(B1[:, :], B1[:, :], AF.Ln),
                       waits=[(vE, mask_state["ve_b1"])])
                mask_state["ae_ln0"] = P.emit(
                    "scalar", lambda s: s.drain(fusable=False), inc=aE)
            elif step == 2:
                P.emit("vector", lambda v: v.scalar_tensor_tensor(
                    B1[:, :], B1[:, :], -0.25, B2[:, :], ALU.mult, ALU.add),
                    waits=[(aE, mask_state["ae_ln0"])])
                mask_state["ve_strip"] = P.emit(
                    "vector", lambda v: v.drain(fusable=False), inc=vE)
            elif step == 3:
                P.emit("scalar", lambda s: s.activation(
                    decayS[:, :], B1[:, :], AF.Exp),
                    waits=[(vE, mask_state["ve_strip"])])
                mask_state["ae_decay"] = P.emit(
                    "scalar", lambda s: s.drain(fusable=False), inc=aE)

        P.emit("vector", lambda v: v.tensor_scalar_mul(bq8S[:, :], bqS[:, :], 0.125),
               waits=[(dBq, ds_bq)])
        ve_bq8 = P.emit("vector", lambda v: v.drain(fusable=False), inc=vE)
        maskscope.close()

        ln_state = {"prev_r2": None}

        def emit_ln_stats(i, src_ap, src_waits):
            c = i
            w0 = list(src_waits)
            P.emit("vector", lambda v, s=src_ap, c=c: v.reduce_sum(
                stat[:, c:c + 1], s, AX.X), waits=w0, inc=vE)
            sqw = list(src_waits)
            if ln_state["prev_r2"] is not None:
                sqw.append((vE, ln_state["prev_r2"]))
            sq = P.emit("gpsimd", lambda g, s=src_ap: g.tensor_mul(
                sqscr[:, :], s, s), waits=sqw, inc=pE)
            r2 = P.emit("vector", lambda v, c=c: v.reduce_sum(
                stat[:, 16 + c:17 + c], sqscr[:, :], AX.X),
                waits=[(pE, sq)], inc=vE)
            ln_state["prev_r2"] = r2
            nm = P.emit("vector", lambda v, c=c: v.tensor_scalar_mul(
                stat[:, 32 + c:33 + c], stat[:, c:c + 1], -1.0 / D),
                waits=[(vE, r2)], inc=vE)
            P.emit("vector", lambda v, c=c: v.tensor_scalar(
                stat[:, 48 + c:49 + c], stat[:, 16 + c:17 + c], 1.0 / D, 1e-5,
                ALU.mult, ALU.add), inc=vE)
            n2 = P.emit("vector", lambda v, c=c: v.tensor_scalar(
                stat[:, 64 + c:65 + c], stat[:, 32 + c:33 + c],
                stat[:, 32 + c:33 + c], -1.0, ALU.mult, ALU.mult),
                waits=[(vE, nm)], inc=vE)
            P.emit("vector", lambda v, c=c: v.tensor_add(
                stat[:, 48 + c:49 + c], stat[:, 48 + c:49 + c],
                stat[:, 64 + c:65 + c]), waits=[(vE, n2)])
            vr = P.emit("vector", lambda v: v.drain(fusable=False), inc=vE)
            return vr

        def emit_ln_lnact(i, vr):
            c = i
            P.emit("scalar", lambda s, c=c: s.activation(
                stat[:, 80 + c:81 + c], stat[:, 48 + c:49 + c], AF.Ln),
                waits=[(vE, vr)], inc=aE)
            ln_ = aE.v
            P.emit("scalar", lambda s, c=c: s.activation(
                stat[:, 80 + c:81 + c], stat[:, 80 + c:81 + c], AF.Exp, scale=-0.5),
                waits=[(aE, ln_)])
            rs = P.emit("scalar", lambda s: s.drain(fusable=False), inc=aE)
            return rs

        def emit_ln_apply(i, src_ap, rs, apply_extra):
            c = i
            sl = (i % 2) * 1024
            apw = [(aE, rs)] + list(apply_extra)
            P.emit("vector", lambda v, s=src_ap, c=c, sl=sl: v.tensor_scalar(
                xnrot[:, sl:sl + 1024], s, stat[:, 32 + c:33 + c],
                stat[:, 80 + c:81 + c], ALU.add, ALU.mult),
                waits=apw)
            ap = P.emit("vector", lambda v: v.drain(fusable=False), inc=vE)
            return ap

        xslot_free = [None, None, None]
        xnrot_tr_gate = [None, None]
        xe_rec = [0] * NT
        lnst = {}
        lnrs = {}
        with ExitStack() as PS1:
            psT = [PS1.enter_context(nc.psum_tensor("psT0", [128, 1024], BF16)),
                   PS1.enter_context(nc.psum_tensor("psT1", [128, 1024], BF16))]
            psT_free = [None, None]
            ds_x = {}
            xsems = {}
            for i in range(NT + 2):
                if i < NT:
                    sl3 = (i % 3) * 1024
                    wslot = [] if xslot_free[i % 3] is None else [xslot_free[i % 3]]
                    xeng = "gpsimd" if i >= 10 else "sync"
                    xsem = dXg[i % 3] if i >= 10 else dX[i % 3]
                    P.emit(xeng, lambda s, i=i, sl3=sl3: s.dma_start(
                        out=xrot[:, sl3:sl3 + 1024],
                        in_=dx[i * 128:(i + 1) * 128, :]),
                        waits=wslot, inc=xsem, k=16)
                    ds_x[i] = xsem.v
                    xsems[i] = xsem
                if 2 <= i and i - 2 < NT:
                    t = i - 2
                    sl3 = (t % 3) * 1024
                    sl = (t % 2) * 1024
                    extra = ([] if xnrot_tr_gate[t % 2] is None
                             else [xnrot_tr_gate[t % 2]])
                    ap = emit_ln_apply(t, xrot[:, sl3:sl3 + 1024], lnrs[t], extra)
                    xslot_free[t % 3] = (vE, ap)
                    tw = [(vE, ap), (dIdent, ds_ident)]
                    if psT_free[t % 2] is not None:
                        tw.append(psT_free[t % 2])
                    for j in range(FT):
                        P.emit("tensor", lambda t_, t=t, j=j, sl=sl: t_.transpose(
                            psT[t % 2][:, j * 128:(j + 1) * 128],
                            xnrot[:, sl + j * 128:sl + (j + 1) * 128], identS[:, :]),
                            waits=(tw if j == 0 else []),
                            inc=(tE if j == FT - 1 else None))
                    trrec = tE.v
                    xnrot_tr_gate[t % 2] = (tE, trrec)
                    dst = ftv(xnT)[:, :, t * 128:(t + 1) * 128]
                    srcT = psT[t % 2][:, :].rearrange("p (a n) -> p a n", a=FT)
                    xe = P.emit("vector", lambda v, dst=dst, srcT=srcT: v.tensor_copy(
                        dst, srcT), waits=[(tE, trrec)], inc=vE)
                    xe_rec[t] = xe
                    psT_free[t % 2] = (vE, xe)
                    if t in (2, 5, 8, 11):
                        mask_step((2, 5, 8, 11).index(t))
                if 1 <= i and i - 1 < NT:
                    lnrs[i - 1] = emit_ln_lnact(i - 1, lnst[i - 1])
                if i < NT:
                    lnst[i] = emit_ln_stats(
                        i, xrot[:, (i % 3) * 1024:(i % 3) * 1024 + 1024],
                        [(xsems[i], ds_x[i])])
        ae_decay = mask_state["ae_decay"]
        P.emit("vector", lambda v: v.drain(fusable=False), inc=vE)
        xnT_all = vE.v
        xnT_chunk = [(xe_rec[4 * c + 3 + 1] if 4 * c + 4 < NT else xnT_all)
                     for c in range(NCH)]

        WfS = TOP.enter_context(nc.sbuf_tensor("WfS", [128, FT * D], BF16))
        attnscope = ExitStack()
        QTs = attnscope.enter_context(nc.sbuf_tensor("QTs", [128, 2 * SEQ], BF16))
        KTs = attnscope.enter_context(nc.sbuf_tensor("KTs", [128, 2 * SEQ], BF16))
        V = attnscope.enter_context(nc.sbuf_tensor("V", [128, NT * 1024], BF16))
        PTb = attnscope.enter_context(nc.sbuf_tensor("PTb", [128, 4 * 1024], BF16))
        Wqp = attnscope.enter_context(nc.sbuf_tensor("Wqp", [128, 2 * 1024], BF16))
        Wkp = attnscope.enter_context(nc.sbuf_tensor("Wkp", [128, 2 * 1024], BF16))
        cmS = attnscope.enter_context(nc.sbuf_tensor("cmS", [2, 16], F32))
        cmbf = attnscope.enter_context(nc.sbuf_tensor("cmbf", [2, 1], BF16))
        negCS = attnscope.enter_context(nc.sbuf_tensor("negCS", [128, 2], F32))
        recipS = attnscope.enter_context(nc.sbuf_tensor("recipS", [128, 512], F32))
        qsqT = attnscope.enter_context(nc.sbuf_tensor("qsqT", [128, SEQ], BF16))
        qsqbf = qsqT[:, :]

        pscope = ExitStack()
        stP = [pscope.enter_context(nc.psum_tensor("stP0", [128, 1024], F32)),
               pscope.enter_context(nc.psum_tensor("stP1", [128, 1024], F32))]
        qkP = [pscope.enter_context(nc.psum_tensor("qk0", [128, 512], F32)),
               pscope.enter_context(nc.psum_tensor("qk1", [128, 512], F32))]
        pool = PsumPool(qkP)

        wslot_gate = [None, None]
        qkslot_gate = [(aE, ae_decay), (aE, ae_decay)]
        dg_w = [None] * PAIRS

        def emit_qk_weights(p):
            pi = p % 2
            wg = [] if wslot_gate[pi] is None else [wslot_gate[pi]]
            src_q = dwq.rearrange("(t p) f -> p t f", p=128)[:, :, p * 128:(p + 1) * 128]
            src_k = dwk.rearrange("(t p) f -> p t f", p=128)[:, :, p * 128:(p + 1) * 128]
            sem_w = dG if pi == 0 else dG2
            P.emit("gpsimd", lambda g, pi=pi, src_q=src_q: g.dma_start(
                out=Wqp[:, pi * 1024:(pi + 1) * 1024].rearrange(
                    "p (t f) -> p t f", t=FT), in_=src_q),
                waits=wg, inc=sem_w, k=16)
            P.emit("gpsimd", lambda g, pi=pi, src_k=src_k: g.dma_start(
                out=Wkp[:, pi * 1024:(pi + 1) * 1024].rearrange(
                    "p (t f) -> p t f", t=FT), in_=src_k),
                inc=sem_w, k=16)
            dg_w[p] = (sem_w, sem_w.v)

        def emit_qk_compute(p):
            pi = p % 2
            last_ev = 0
            last_mm = 0
            for (which, Wt) in (("q", Wqp), ("k", Wkp)):
                for c in range(NCH):
                    j, pw = pool.acquire()
                    for tin in range(FT):
                        w = []
                        if tin == 0:
                            w = [(vE, xnT_chunk[c]), dg_w[p]] + pw
                            if qkslot_gate[pi] is not None:
                                w.append(qkslot_gate[pi])
                        P.emit("tensor", lambda t, j=j, Wt=Wt, tin=tin, pi=pi, c=c:
                               t.matmul(qkP[j][:, 0:512],
                                        Wt[:, pi * 1024 + tin * 128:pi * 1024 + (tin + 1) * 128],
                                        ftv(xnT)[:, tin, c * 512:(c + 1) * 512],
                                        start=(tin == 0), stop=(tin == FT - 1)),
                               waits=w, inc=(tE if tin == FT - 1 else None))
                    last_mm = tE.v
                    dstT = QTs if which == "q" else KTs
                    scale = 0.125 if which == "q" else 1.0
                    bias = bq8S if which == "q" else bkS
                    er = P.emit("scalar", lambda s, dstT=dstT, pi=pi, c=c, j=j,
                                scale=scale, bias=bias, p=p:
                                s.activation(
                                    dstT[:, pi * SEQ + c * 512:pi * SEQ + (c + 1) * 512],
                                    qkP[j][:, 0:512], AF.Identity,
                                    bias=bias[:, p:p + 1], scale=scale),
                                waits=[(tE, last_mm), (dBk, ds_bk), (vE, ve_bq8)],
                                inc=aE)
                    last_ev = er
                    pool.release(j, aE, er)
                    yield
            wslot_gate[pi] = (tE, last_mm)
            red = 0
            for (kk, (which, srcT)) in enumerate((("q", QTs), ("k", KTs))):
                P.emit("vector", lambda v, srcT=srcT, pi=pi: v.tensor_mul(
                    qsqbf[:, 0:1024], srcT[:, pi * SEQ:pi * SEQ + 1024],
                    srcT[:, pi * SEQ:pi * SEQ + 1024]),
                    waits=[(aE, last_ev)])
                yield
                P.emit("vector", lambda v, srcT=srcT, pi=pi: v.tensor_mul(
                    qsqbf[:, 1024:2048],
                    srcT[:, pi * SEQ + 1024:pi * SEQ + 2048],
                    srcT[:, pi * SEQ + 1024:pi * SEQ + 2048]))
                sq = P.emit("vector", lambda v: v.drain(fusable=False), inc=vE)
                yield
                for c in range(NCH):
                    j, pw = pool.acquire()
                    mm = P.emit("tensor", lambda t, j=j, c=c: t.matmul(
                        qkP[j][0:2, 0:512], hselS[:, :],
                        qsqbf[:, c * 512:(c + 1) * 512], start=True, stop=True),
                        waits=[(vE, sq), (dHsel, ds_hsel)] + pw, inc=tE)
                    red = P.emit("vector", lambda v, j=j, cc=kk * 4 + c:
                                 v.reduce_max(cmS[:, cc:cc + 1],
                                              qkP[j][0:2, 0:512], AX.X),
                                 waits=[(tE, mm)], inc=vE)
                    pool.release(j, vE, red)
                    yield
            qm = P.emit("vector", lambda v: v.reduce_max(
                cmS[:, 8:9], cmS[:, 0:4], AX.X), waits=[(vE, red)], inc=vE)
            km = P.emit("vector", lambda v: v.reduce_max(
                cmS[:, 9:10], cmS[:, 4:8], AX.X), waits=[(vE, qm)], inc=vE)
            P.emit("vector", lambda v: v.tensor_mul(
                cmS[:, 10:11], cmS[:, 8:9], cmS[:, 9:10]), waits=[(vE, km)])
            pr_ = P.emit("vector", lambda v: v.drain(fusable=False), inc=vE)
            yield
            l1 = P.emit("scalar", lambda s: s.activation(
                cmS[:, 11:12], cmS[:, 10:11], AF.Ln), waits=[(vE, pr_)], inc=aE)
            P.emit("scalar", lambda s: s.activation(
                cmS[:, 11:12], cmS[:, 11:12], AF.Exp, scale=0.5),
                waits=[(aE, l1)])
            e1 = P.emit("scalar", lambda s: s.drain(fusable=False), inc=aE)
            yield
            P.emit("vector", lambda v: v.tensor_copy(cmbf[:, :], cmS[:, 11:12]),
                   waits=[(aE, e1)])
            cb = P.emit("vector", lambda v: v.drain(fusable=False), inc=vE)
            yield
            j, pw = pool.acquire()
            rm = P.emit("tensor", lambda t, j=j: t.matmul(
                qkP[j][:, 0:1], hselTS[:, :], cmbf[:, :], start=True, stop=True),
                waits=[(vE, cb)] + pw, inc=tE)
            ng = P.emit("scalar", lambda s, j=j, pi=pi: s.activation(
                negCS[:, pi:pi + 1], qkP[j][:, 0:1], AF.Copy, scale=-0.5),
                waits=[(tE, rm)], inc=aE)
            pool.release(j, aE, ng)
            negc[p] = ng
            if DEBUG_TAPS and p == PAIRS - 1:
                P.emit("gpsimd", lambda g: g.dma_start(
                    out=taps["t_qt"], in_=QTs[:, :]),
                    waits=[(aE, ng)], inc=dG, k=16)
                P.emit("gpsimd", lambda g: g.dma_start(
                    out=taps["t_kt"], in_=KTs[:, :]), inc=dG, k=16)
                P.emit("gpsimd", lambda g: g.dma_start(
                    out=taps["t_negc"], in_=negCS[:, :]), inc=dG, k=16)

        emit_qk_weights(0)
        emit_qk_weights(1)
        negc = [None] * PAIRS
        for _ in emit_qk_compute(0):
            pass

        v_done = 0
        with ExitStack() as WVS:
            WvS = WVS.enter_context(nc.sbuf_tensor("WvS", [128, FT * 512], BF16))
            for half in range(2):
                lastmm_for_dma = None
                if half == 1:
                    lastmm_for_dma = (tE, tE.v)
                P.emit("gpsimd", lambda g, half=half: g.dma_start(
                    out=WvS[:, :].rearrange("p (t f) -> p t f", t=FT),
                    in_=dwv.rearrange("(t p) f -> p t f", p=128)[:, :, half * 512:(half + 1) * 512]),
                    waits=([lastmm_for_dma] if lastmm_for_dma else []), inc=dWv, k=16)
                dg_wv = dWv.v
                for mt in range(NT):
                    j, pw = pool.acquire()
                    for tin in range(FT):
                        w = []
                        if tin == 0:
                            w = [(vE, xnT_all), (dWv, dg_wv)] + pw
                        P.emit("tensor", lambda t, j=j, mt=mt, half=half, tin=tin:
                               t.matmul(qkP[j][:, 0:512],
                                        ftv(xnT)[:, tin, mt * 128:(mt + 1) * 128],
                                        WvS[:, tin * 512:(tin + 1) * 512],
                                        start=(tin == 0), stop=(tin == FT - 1)),
                               waits=w, inc=(tE if tin == FT - 1 else None))
                    mmrec = tE.v
                    evw = [(tE, mmrec)]
                    if mt == 0 and half == 0:
                        evw.append((aE, ae_decay))
                    ev = P.emit("vector", lambda v, j=j, mt=mt, half=half:
                                v.tensor_copy(
                                    V[:, mt * 1024 + half * 512:mt * 1024 + (half + 1) * 512],
                                    qkP[j][:, 0:512]),
                                waits=evw, inc=vE)
                    v_done = ev
                    pool.release(j, vE, ev)
            P.emit("vector", lambda v: v.drain(fusable=False), inc=vE)
            v_done = vE.v
            v_mm_done = tE.v

        gscope = ExitStack()
        Wacc = gscope.enter_context(nc.sbuf_tensor("Wacc", [128, 2048], F32))
        g1b = gscope.enter_context(nc.sbuf_tensor("g1b", [128, 512], F32))
        g2b = gscope.enter_context(nc.sbuf_tensor("g2b", [128, 512], F32))

        wb_units = [(part, c, pt) for part in range(2) for c in range(4)
                    for pt in range(4)]
        wb_state = {"gfree": (tE, v_mm_done), "idx": 0, "done": None}

        def emit_wbuild_unit():
            if wb_state["idx"] >= len(wb_units):
                return
            part, c, pt = wb_units[wb_state["idx"]]
            wb_state["idx"] += 1
            gi = part * 4 + c
            wg = [] if wb_state["gfree"] is None else [wb_state["gfree"]]
            P.emit("sync", lambda s, gi=gi, pt=pt: s.dma_start(
                out=g1b[:, :], in_=dg1[gi, pt * 128:(pt + 1) * 128, :]),
                waits=wg, inc=dGrep, k=16)
            P.emit("sync", lambda s, gi=gi, pt=pt: s.dma_start(
                out=g2b[:, :], in_=dg2[gi, pt * 128:(pt + 1) * 128, :]),
                inc=dGrep, k=16)
            ds_g = dGrep.v
            if c == 0:
                u = P.emit("gpsimd", lambda g, pt=pt: g.tensor_mul(
                    Wacc[:, pt * 512:(pt + 1) * 512], g1b[:, :], g2b[:, :]),
                    waits=[(dGrep, ds_g)], inc=pE)
                wb_state["gfree"] = (pE, u)
            else:
                u1 = P.emit("gpsimd", lambda g: g.tensor_mul(
                    g1b[:, :], g1b[:, :], g2b[:, :]), waits=[(dGrep, ds_g)], inc=pE)
                u2 = P.emit("gpsimd", lambda g, pt=pt: g.tensor_add(
                    Wacc[:, pt * 512:(pt + 1) * 512],
                    Wacc[:, pt * 512:(pt + 1) * 512], g1b[:, :]),
                    waits=[(pE, u1)], inc=pE)
                wb_state["gfree"] = (pE, u2)
            if wb_state["idx"] == len(wb_units) // 2:
                last = 0
                for pt in range(4):
                    last = P.emit("gpsimd", lambda g, pt=pt: g.tensor_copy(
                        WfS[:, pt * D:pt * D + 512],
                        Wacc[:, pt * 512:(pt + 1) * 512]),
                        waits=([wb_state["gfree"]] if pt == 0 else []), inc=pE)
                    last = P.emit("gpsimd", lambda g, pt=pt: g.tensor_copy(
                        WfS[:, (4 + pt) * D + 512:(5 + pt) * D],
                        Wacc[:, pt * 512:(pt + 1) * 512]), inc=pE)
                wb_state["gfree"] = (pE, last)
            if wb_state["idx"] == len(wb_units):
                last = 0
                for pt in range(4):
                    last = P.emit("gpsimd", lambda g, pt=pt: g.tensor_copy(
                        WfS[:, pt * D + 512:(pt + 1) * D],
                        Wacc[:, pt * 512:(pt + 1) * 512]),
                        waits=([wb_state["gfree"]] if pt == 0 else []), inc=pE)
                    last = P.emit("gpsimd", lambda g, pt=pt: g.tensor_scalar_mul(
                        WfS[:, (4 + pt) * D:(4 + pt) * D + 512],
                        Wacc[:, pt * 512:(pt + 1) * 512], -1.0), inc=pE)
                last = P.emit("gpsimd", lambda g: g.drain(fusable=False), inc=pE)
                wb_state["done"] = last

        gstep = 0
        dbg_state = {}
        gav = {}
        exp_by_j = {0: None, 1: None}
        dg_wo = None
        with ExitStack() as AVS:
            pv = AVS.enter_context(nc.psum_tensor("pv0", [128, 512], F32))
            pr = AVS.enter_context(nc.psum_tensor("pr0", [128, 512], F32))
            aev_q = [None, None, None, None]
            pvpr_free = None
            aev_last = 0

            def gen_pair_filler(p):
                if p + 2 < PAIRS:
                    emit_qk_weights(p + 2)
                yield
                if p + 1 < PAIRS:
                    yield from emit_qk_compute(p + 1)
                for _ in range(5):
                    emit_wbuild_unit()
                    yield

            for p in range(PAIRS):
                pi = p % 2
                ng = negc[p]
                stp_wait = {0: ([(aE, exp_by_j[0])] if exp_by_j[0] else []),
                            1: ([(aE, exp_by_j[1])] if exp_by_j[1] else [])}
                steps = [(c, mt) for c in range(NCH) for mt in range(4 * c + 4)]
                S = len(steps)
                g0 = gstep
                strec = {}
                er = {}
                erh = {}
                mrr = {}

                def emit_ST(k):
                    c, mt = steps[k]
                    j = k % 2
                    no = max(0, 128 * mt - 512 * c)
                    if k < 2:
                        w = [(aE, ng)] + stp_wait[j]
                    elif last_av[0] < k - 2:
                        w = [(aE, er[k - 2])]
                    else:
                        w = []
                    P.emit("tensor", lambda t, j=j, pi=pi, mt=mt, c=c, no=no: t.matmul(
                        stP[j][:, no:512],
                        KTs[0:64, pi * SEQ + mt * 128:pi * SEQ + (mt + 1) * 128],
                        QTs[0:64, pi * SEQ + c * 512 + no:pi * SEQ + (c + 1) * 512],
                        start=True, stop=True, tile_position=(0, 0)),
                        waits=w)
                    P.emit("tensor", lambda t, j=j, pi=pi, mt=mt, c=c, no=no: t.matmul(
                        stP[j][:, 512 + no:1024],
                        KTs[64:128, pi * SEQ + mt * 128:pi * SEQ + (mt + 1) * 128],
                        QTs[64:128, pi * SEQ + c * 512 + no:pi * SEQ + (c + 1) * 512],
                        start=True, stop=True, tile_position=(64, 0)),
                        inc=(None if PARANOID else tE))
                    if PARANOID:
                        P.emit("tensor", lambda t: t.drain(fusable=False), inc=tE)
                    strec[k] = tE.v

                def emit_exp(k):
                    c, mt = steps[k]
                    j = k % 2
                    no = max(0, 128 * mt - 512 * c)
                    slot = (g0 + k) % 4
                    we = [(tE, strec[k])]
                    if (g0 + k - 4) in gav:
                        we.append((tE, gav[g0 + k - 4]))
                    if no == 0:
                        e_ = P.emit("scalar", lambda s, j=j, slot=slot, pi=pi:
                                    s.activation(
                                        PTb[:, slot * 1024:(slot + 1) * 1024],
                                        stP[j][:, 0:1024], AF.Exp,
                                        bias=negCS[:, pi:pi + 1]),
                                    waits=we, inc=aE)
                        eh = e_
                    else:
                        eh = P.emit("scalar", lambda s, j=j, slot=slot, pi=pi, no=no:
                                    s.activation(PTb[:, slot * 1024 + no:slot * 1024 + 512],
                                                 stP[j][:, no:512], AF.Exp,
                                                 bias=negCS[:, pi:pi + 1]),
                                    waits=we, inc=aE)
                        e_ = P.emit("scalar", lambda s, j=j, slot=slot, pi=pi, no=no:
                                    s.activation(PTb[:, slot * 1024 + 512 + no:(slot + 1) * 1024],
                                                 stP[j][:, 512 + no:1024], AF.Exp,
                                                 bias=negCS[:, pi:pi + 1]),
                                    inc=aE)
                    er[k] = e_
                    erh[k] = eh
                    stp_wait[j] = [(aE, e_)]
                    exp_by_j[j] = e_

                def emit_mult(k):
                    c, mt = steps[k]
                    slot = (g0 + k) % 4
                    no = max(0, 128 * mt - 512 * c)
                    off = 384 + c * 512 - mt * 128
                    P.emit("vector", lambda v, slot=slot, off=off, no=no: v.tensor_mul(
                        PTb[:, slot * 1024 + no:slot * 1024 + 512],
                        PTb[:, slot * 1024 + no:slot * 1024 + 512],
                        decayS[:, off + no:off + 512]),
                        waits=[(aE, erh[k]), (aE, ae_decay)])
                    m_ = P.emit("vector", lambda v, slot=slot, off=off, no=no: v.tensor_mul(
                        PTb[:, slot * 1024 + 512 + no:(slot + 1) * 1024],
                        PTb[:, slot * 1024 + 512 + no:(slot + 1) * 1024],
                        decayS[:, off + no:off + 512]),
                        waits=[(aE, er[k])], inc=vE)
                    mrr[k] = m_

                def emit_attnV(k, pgate=None):
                    nonlocal aev_last, pvpr_free
                    c, mt = steps[k]
                    nmt = 4 * c + 4
                    slot = (g0 + k) % 4
                    no = max(0, 128 * mt - 512 * c)
                    wv_ = [(vE, pgate if pgate is not None else mrr[k])]
                    if mt == 0:
                        wv_.append((vE, v_done))
                        wv_.append((vE, ve_ones))
                        if pvpr_free is not None:
                            wv_.append(pvpr_free)
                    h0, h1 = 2 * p, 2 * p + 1
                    islast = (mt == nmt - 1)
                    P.emit("tensor", lambda t, mt=mt, h0=h0, slot=slot,
                           islast=islast, no=no:
                           t.matmul(pv[0:64, no:512],
                                    V[:, mt * 1024 + h0 * 64:mt * 1024 + h0 * 64 + 64],
                                    PTb[:, slot * 1024 + no:slot * 1024 + 512],
                                    start=(mt == 0), stop=islast,
                                    tile_position=(0, 0), skip_group_check=True),
                           waits=wv_)
                    P.emit("tensor", lambda t, mt=mt, h1=h1, slot=slot,
                           islast=islast, no=no:
                           t.matmul(pv[64:128, no:512],
                                    V[:, mt * 1024 + h1 * 64:mt * 1024 + h1 * 64 + 64],
                                    PTb[:, slot * 1024 + 512 + no:(slot + 1) * 1024],
                                    start=(mt == 0), stop=islast,
                                    tile_position=(0, 64), skip_group_check=True))
                    P.emit("tensor", lambda t, mt=mt, slot=slot,
                           islast=islast, no=no:
                           t.matmul(pr[0:64, no:512], ones64[:, :],
                                    PTb[:, slot * 1024 + no:slot * 1024 + 512],
                                    start=(mt == 0), stop=islast,
                                    tile_position=(0, 0), skip_group_check=True))
                    P.emit("tensor", lambda t, mt=mt, slot=slot,
                           islast=islast, no=no:
                           t.matmul(pr[64:128, no:512], ones64[:, :],
                                    PTb[:, slot * 1024 + 512 + no:(slot + 1) * 1024],
                                    start=(mt == 0), stop=islast,
                                    tile_position=(0, 64), skip_group_check=True),
                           inc=(None if (PARANOID and islast) else tE))
                    if PARANOID and islast:
                        P.emit("tensor", lambda t: t.drain(fusable=False), inc=tE)
                    gav[g0 + k] = tE.v
                    if (QUARTER_NORM and mt >= 4 * c) or \
                       (not QUARTER_NORM and islast):
                        if QUARTER_NORM:
                            q = mt - 4 * c
                            lo, wd = 128 * q, 128
                        else:
                            q = 3
                            lo, wd = 0, 512
                        h = wd // 2
                        rcw = [(tE, gav[g0 + k])]
                        if aev_q[q] is not None:
                            rcw.append(aev_q[q])
                        rl = P.emit("scalar", lambda s, lo=lo, wd=wd: s.activation(
                            recipS[:, lo:lo + wd], pr[:, lo:lo + wd], AF.Ln),
                            waits=rcw, inc=aE)
                        ra = P.emit("scalar", lambda s, lo=lo, h=h: s.activation(
                            recipS[:, lo:lo + h], recipS[:, lo:lo + h],
                            AF.Exp, scale=-1.0), waits=[(aE, rl)], inc=aE)
                        rb = P.emit("scalar", lambda s, lo=lo, h=h: s.activation(
                            recipS[:, lo + h:lo + 2 * h],
                            recipS[:, lo + h:lo + 2 * h],
                            AF.Exp, scale=-1.0), inc=aE)
                        base = p * SEQ + c * 512 + lo
                        P.emit("vector", lambda v, lo=lo, base=base, h=h: v.tensor_mul(
                            attn_outT[:, base:base + h],
                            pv[:, lo:lo + h], recipS[:, lo:lo + h]),
                            waits=[(aE, ra)])
                        ae_ = P.emit("vector", lambda v, lo=lo, base=base, h=h: v.tensor_mul(
                            attn_outT[:, base + h:base + 2 * h],
                            pv[:, lo + h:lo + 2 * h], recipS[:, lo + h:lo + 2 * h]),
                            waits=[(aE, rb)], inc=vE)
                        aev_q[q] = (vE, ae_)
                        aev_last = ae_
                        if q == 3:
                            pvpr_free = (vE, ae_)
                    if islast and DEBUG_TAPS and p == PAIRS - 1 and c == 0:
                        d1 = P.emit("vector", lambda v: v.tensor_copy(
                            sqscr[:, 0:512], pv[:, :]),
                            waits=[(tE, gav[g0 + k])], inc=vE)
                        d2 = P.emit("vector", lambda v: v.tensor_copy(
                            sqscr[:, 512:1024], pr[:, :]),
                            waits=[(vE, d1)], inc=vE)
                        pvpr_free = (vE, d2)
                        P.emit("sync", lambda s: s.dma_start(
                            out=taps["t_pvpr"], in_=sqscr[:, :]),
                            waits=[(vE, d2)], inc=dS, k=16)
                        P.emit("gpsimd", lambda g: g.dma_start(
                            out=taps["t_v"], in_=V[:, :]), inc=dG, k=16)

                av_sched = {}
                prev_it = -1
                for kk in range(S):
                    mt0_ = steps[kk][1]
                    des = kk + (4 if mt0_ == 0 else 2)
                    it = max(des, prev_it)
                    prev_it = it
                    av_sched.setdefault(it, []).append(kk)
                last_av = [-1]
                filler = gen_pair_filler(p)
                for k in range(S):
                    if INTERLEAVE_QK:
                        next(filler, None)
                    for kk in av_sched.get(k, []):
                        emit_attnV(kk)
                        last_av[0] = kk
                    emit_ST(k)
                    emit_exp(k)
                    emit_mult(k)
                for it in range(S, prev_it + 1):
                    for kk in av_sched.get(it, []):
                        emit_attnV(kk)
                        last_av[0] = kk
                for _ in filler:
                    pass
                gstep = g0 + S
                qkslot_gate[pi] = (tE, gav[gstep - 1])
            attn_done_t = tE.v
            P.emit("vector", lambda v: v.drain(fusable=False),
                   waits=[(vE, aev_last)], inc=vE)
            attn_done_v = vE.v

        pscope.close()
        gscope.close()
        attnscope.close()

        WoS = TOP.enter_context(nc.sbuf_tensor("WoS", [128, FT * D], BF16))
        P.emit("gpsimd", lambda g: g.dma_start(
            out=WoS[:, :].rearrange("p (t f) -> p t f", t=FT),
            in_=dwo.rearrange("(t p) f -> p t f", p=128)),
            waits=[(tE, attn_done_t)], inc=dWo, k=16)
        dg_wo = dWo.v
        boS = TOP.enter_context(nc.sbuf_tensor("boS", [1, D], F32))
        bfHi = TOP.enter_context(nc.sbuf_tensor("bfHi", [1, D], BF16))
        bfLo = TOP.enter_context(nc.sbuf_tensor("bfLo", [1, D], BF16))
        with nc.sbuf_tensor("bfS", [1, D], F32) as bfS:
            P.emit("sync", lambda s: s.dma_start(out=bfS[:, :], in_=dbf),
                   waits=[(tE, attn_done_t)], inc=dBf, k=16)
            ds_bf = dBf.v
            P.emit("vector", lambda v, bfS=bfS: v.tensor_copy(bfHi[:, :], bfS[:, :]),
                   waits=[(dBf, ds_bf), (vE, attn_done_v)], inc=vE)
            ve_bfhi = vE.v
            P.emit("vector", lambda v, bfS=bfS: v.tensor_sub(
                bfLo[:, :], bfS[:, :], bfHi[:, :]), waits=[(vE, ve_bfhi)])
            ve_bflo = P.emit("vector", lambda v: v.drain(fusable=False), inc=vE)
        P.emit("sync", lambda s: s.dma_start(out=boS[:, :], in_=dbo),
               waits=[(tE, attn_done_t)], inc=dBo, k=16)
        ds_bo = dBo.v
        x2 = TOP.enter_context(nc.sbuf_tensor("x2", [128, NT * 1024], F32))
        outbuf = TOP.enter_context(nc.sbuf_tensor("outbuf", [128, 2 * 1024], F32))
        wf_done = wb_state["done"]

        with ExitStack() as PSB:
            with ExitStack() as BOS:
                psBO = BOS.enter_context(nc.psum_tensor("psBO", [1, 1024], F32))
                for ch in range(2):
                    for tin in range(FT):
                        w = []
                        if tin == 0 and ch == 0:
                            w = [(dWo, dg_wo), (dBv, dg_bv), (tE, attn_done_t)]
                        P.emit("tensor", lambda t, ch=ch, tin=tin: t.matmul(
                            psBO[0:1, ch * 512:(ch + 1) * 512], bvS[:, tin:tin + 1],
                            WoS[:, tin * D + ch * 512:tin * D + (ch + 1) * 512],
                            start=(tin == 0), stop=(tin == FT - 1)),
                            waits=w,
                            inc=(tE if (tin == FT - 1 and ch == 1) else None))
                bo_mm = tE.v
                b1_ = P.emit("vector", lambda v: v.tensor_add(
                    boS[:, :], psBO[0:1, :], boS[:, :]),
                    waits=[(tE, bo_mm), (dBo, ds_bo), (pE, wf_done),
                           (vE, attn_done_v)], inc=vE)
                b2_ = P.emit("vector", lambda v: v.tensor_copy(
                    boHi[:, :], boS[:, :]), waits=[(vE, b1_)], inc=vE)
                P.emit("vector", lambda v: v.tensor_sub(
                    boLo[:, :], boS[:, :], boHi[:, :]), waits=[(vE, b2_)])
                bo_ready = P.emit("vector", lambda v: v.drain(fusable=False), inc=vE)

            psW = [PSB.enter_context(nc.psum_tensor("psW0", [128, 512], F32)),
                   PSB.enter_context(nc.psum_tensor("psW1", [128, 512], F32))]
            psT2 = [PSB.enter_context(nc.psum_tensor("psT2a", [128, 1024], BF16)),
                    PSB.enter_context(nc.psum_tensor("psT2b", [128, 1024], BF16))]
            psF = [PSB.enter_context(nc.psum_tensor("psF0", [128, 1024], F32)),
                   PSB.enter_context(nc.psum_tensor("psF1", [128, 1024], F32))]
            wpool = PsumPool(psW)
            psT2_free = [None, None]
            psF_free = [None, None]
            resid1 = [0] * NT
            x2e_rec = [0] * NT
            out_dma = {}
            dSg = Sem(TOP, nc, "dSg")
            ffn_ev = [0] * NT

            def stageA(i):
                sl = (i % 2) * 1024
                wslot = [xslot_free[i % 2]] if xslot_free[i % 2] is not None else []
                P.emit("sync", lambda s, i=i, sl=sl: s.dma_start(
                    out=xrot[:, sl:sl + 1024], in_=dx[i * 128:(i + 1) * 128, :]),
                    waits=wslot, inc=dX[i % 2], k=16)
                ds_x2 = (dX[i % 2], dX[i % 2].v)
                for ch in range(2):
                    j, pw = wpool.acquire()
                    for tin in range(FT):
                        w = []
                        if tin == 0:
                            w = [(tE, attn_done_t), (vE, attn_done_v),
                                 (dWo, dg_wo), (vE, bo_ready)] + pw
                        P.emit("tensor", lambda t, j=j, i=i, ch=ch, tin=tin:
                               t.matmul(psW[j][:, :],
                                        attn_outT[:, tin * SEQ + i * 128:tin * SEQ + (i + 1) * 128],
                                        WoS[:, tin * D + ch * 512:tin * D + (ch + 1) * 512],
                                        start=(tin == 0), stop=False),
                               waits=w)
                    P.emit("tensor", lambda t, j=j, ch=ch: t.matmul(
                        psW[j][:, :], ones1[:, :],
                        boHi[0:1, ch * 512:(ch + 1) * 512],
                        start=False, stop=False))
                    P.emit("tensor", lambda t, j=j, ch=ch: t.matmul(
                        psW[j][:, :], ones1[:, :],
                        boLo[0:1, ch * 512:(ch + 1) * 512],
                        start=False, stop=True), inc=tE)
                    mmrec = tE.v
                    rv = P.emit("vector", lambda v, j=j, i=i, ch=ch, sl=sl:
                                v.tensor_add(
                                    x2[:, i * 1024 + ch * 512:i * 1024 + (ch + 1) * 512],
                                    psW[j][:, :],
                                    xrot[:, sl + ch * 512:sl + (ch + 1) * 512]),
                                waits=[(tE, mmrec), ds_x2], inc=vE)
                    wpool.release(j, vE, rv)
                    resid1[i] = rv
                xslot_free[i % 2] = (vE, resid1[i])

            ln2st = {}
            ln2rs = {}

            def stageB_stats(i):
                ln2st[i] = emit_ln_stats(
                    i, x2[:, i * 1024:(i + 1) * 1024], [(vE, resid1[i])])

            def stageB_ln(i):
                ln2rs[i] = emit_ln_lnact(i, ln2st[i])

            def stageB(i):
                sl = (i % 2) * 1024
                extra = [] if xnrot_tr_gate[i % 2] is None else [xnrot_tr_gate[i % 2]]
                ap = emit_ln_apply(i, x2[:, i * 1024:(i + 1) * 1024], ln2rs[i], extra)
                tw = [(vE, ap)]
                if psT2_free[i % 2] is not None:
                    tw.append(psT2_free[i % 2])
                for j in range(FT):
                    P.emit("tensor", lambda t, i=i, j=j, sl=sl: t.transpose(
                        psT2[i % 2][:, j * 128:(j + 1) * 128],
                        xnrot[:, sl + j * 128:sl + (j + 1) * 128], identS[:, :]),
                        waits=(tw if j == 0 else []),
                        inc=(tE if j == FT - 1 else None))
                trrec = tE.v
                xnrot_tr_gate[i % 2] = (tE, trrec)
                dst = ftv(xnT)[:, :, i * 128:(i + 1) * 128]
                srcT = psT2[i % 2][:, :].rearrange("p (a n) -> p a n", a=FT)
                xe = P.emit("vector", lambda v, dst=dst, srcT=srcT: v.tensor_copy(
                    dst, srcT), waits=[(tE, trrec)], inc=vE)
                x2e_rec[i] = xe
                psT2_free[i % 2] = (vE, xe)

            def stageC(i):
                j = i % 2
                for ch in range(2):
                    for tin in range(FT):
                        w = []
                        if tin == 0:
                            if ch == 0:
                                w = [(vE, x2e_rec[i]), (pE, wf_done)]
                                if psF_free[j] is not None:
                                    w.append(psF_free[j])
                        P.emit("tensor", lambda t, j=j, i=i, ch=ch, tin=tin:
                               t.matmul(psF[j][:, ch * 512:(ch + 1) * 512],
                                        ftv(xnT)[:, tin, i * 128:(i + 1) * 128],
                                        WfS[:, tin * D + ch * 512:tin * D + (ch + 1) * 512],
                                        start=(tin == 0), stop=False),
                               waits=w)
                    P.emit("tensor", lambda t, j=j, ch=ch: t.matmul(
                        psF[j][:, ch * 512:(ch + 1) * 512], ones1[:, :],
                        bfHi[0:1, ch * 512:(ch + 1) * 512],
                        start=False, stop=False),
                        waits=([(vE, ve_bflo)] if (i == 0 and ch == 0) else []))
                    P.emit("tensor", lambda t, j=j, ch=ch: t.matmul(
                        psF[j][:, ch * 512:(ch + 1) * 512], ones1[:, :],
                        bfLo[0:1, ch * 512:(ch + 1) * 512],
                        start=False, stop=True),
                        inc=(tE if ch == 1 else None))
                mmrec = tE.v
                wev = [(tE, mmrec)]
                if i >= 2 and (i // 2 - 1) in out_dma:
                    wev.append(out_dma[i // 2 - 1])
                ev = P.emit("vector", lambda v, j=j, i=i: v.tensor_add(
                    outbuf[:, (i % 2) * 1024:(i % 2 + 1) * 1024], psF[j][:, :],
                    x2[:, i * 1024:(i + 1) * 1024]),
                    waits=wev, inc=vE)
                ffn_ev[i] = ev
                psF_free[j] = (vE, ev)
                if i % 2 == 1:
                    g = i // 2
                    dst = dout[g * 256:(g + 1) * 256, :].rearrange(
                        "(t p) f -> p t f", p=128)
                    srcb = outbuf[:, :].rearrange("p (t f) -> p t f", t=2)
                    wdr = P.emit("vector", lambda v: v.drain(fusable=False),
                                 waits=[(vE, ffn_ev[i])], inc=vE)
                    qeng, qsem = (("sync", dS) if g % 2 == 0
                                  else ("gpsimd", dSg))
                    P.emit(qeng, lambda s, dst=dst, srcb=srcb: s.dma_start(
                        out=dst, in_=srcb), waits=[(vE, wdr)], inc=qsem, k=16)
                    out_dma[g] = (qsem, qsem.v)

            for i in range(NT + 4):
                if 4 <= i and i - 4 < NT:
                    stageC(i - 4)
                if 3 <= i and i - 3 < NT:
                    stageB(i - 3)
                if 2 <= i and i - 2 < NT:
                    stageB_ln(i - 2)
                if 1 <= i and i - 1 < NT:
                    stageB_stats(i - 1)
                if i < NT:
                    stageA(i)

            P.emit("sync", lambda s: s.nop(),
                   waits=[out_dma[NT // 2 - 2], out_dma[NT // 2 - 1]])
            if DEBUG_TAPS:
                P.emit("gpsimd", lambda g: g.dma_start(out=taps["t_attn"], in_=attn_outT[:, :]),
                       waits=[out_dma[NT // 2 - 2], out_dma[NT // 2 - 1]], inc=dG, k=16)
                P.emit("gpsimd", lambda g: g.dma_start(out=taps["t_xnT"], in_=xnT[:, :]), inc=dG, k=16)
                P.emit("gpsimd", lambda g: g.dma_start(out=taps["t_wf"], in_=WfS[:, :]), inc=dG, k=16)
                P.emit("gpsimd", lambda g: g.dma_start(out=taps["t_decay"], in_=decayS[:, :]), inc=dG, k=16)
                P.emit("sync", lambda s: s.dma_start(out=taps["t_x2"], in_=x2[:, :]), inc=dS, k=16)
                fin = dS.v
                P.emit("gpsimd", lambda g: g.nop(), waits=[(dG, dG.v), (dS, fin)])

        blk = TOP.enter_context(nc.Block())

        @blk.sync
        def _(e):
            for op in P.ops["sync"]:
                op(e)

        @blk.gpsimd
        def _(e):
            for op in P.ops["gpsimd"]:
                op(e)

        @blk.vector
        def _(e):
            for op in P.ops["vector"]:
                op(e)

        @blk.scalar
        def _(e):
            for op in P.ops["scalar"]:
                op(e)

        @blk.tensor
        def _(e):
            for op in P.ops["tensor"]:
                op(e)

    return nc


_NC_CACHE = {}


def _get_program():
    if "nc" not in _NC_CACHE:
        _NC_CACHE["nc"] = build_program()
    return _NC_CACHE["nc"]


def kernel(L0_attn_x, L0_attn_Wq, L0_attn_bq, L0_attn_Wk, L0_attn_bk,
           L0_attn_Wv, L0_attn_bv, L0_attn_Wo, L0_attn_bo,
           L0_ffn_G1_R, L0_ffn_G2_R, L0_ffn_G1_I, L0_ffn_G2_I,
           L0_ffn_bias_real, L0_ffn_bias_imag):
    x = np.ascontiguousarray(np.asarray(L0_attn_x, dtype=np.float32))
    wq = np.ascontiguousarray(np.asarray(L0_attn_Wq, dtype=np.float32))
    wk = np.ascontiguousarray(np.asarray(L0_attn_Wk, dtype=np.float32))
    wv = np.ascontiguousarray(np.asarray(L0_attn_Wv, dtype=np.float32))
    wo = np.ascontiguousarray(np.asarray(L0_attn_Wo, dtype=np.float32))
    bq = np.ascontiguousarray(np.asarray(L0_attn_bq, dtype=np.float32))
    bk = np.ascontiguousarray(np.asarray(L0_attn_bk, dtype=np.float32))
    bv = np.ascontiguousarray(np.asarray(L0_attn_bv, dtype=np.float32))
    bo = np.ascontiguousarray(np.asarray(L0_attn_bo, dtype=np.float32)).reshape(1, D)
    bfn = np.ascontiguousarray(np.concatenate([
        np.asarray(L0_ffn_bias_real, dtype=np.float32),
        np.asarray(L0_ffn_bias_imag, dtype=np.float32)]).reshape(1, D))

    def reps(G1, G2):
        g1l, g2l = [], []
        for c in range(4):
            X = np.asarray(G1, dtype=np.float32)[0, :, c, :]
            Y = np.asarray(G2, dtype=np.float32)[c, :, 0, :]
            g1l.append(np.ascontiguousarray(np.broadcast_to(
                X.T[:, None, :, None], (32, 16, 32, 16)).reshape(512, 512)))
            g2l.append(np.ascontiguousarray(np.broadcast_to(
                Y.T[None, :, None, :], (32, 16, 32, 16)).reshape(512, 512)))
        return g1l, g2l

    g1R, g2R = reps(L0_ffn_G1_R, L0_ffn_G2_R)
    g1I, g2I = reps(L0_ffn_G1_I, L0_ffn_G2_I)
    g1rep = np.ascontiguousarray(np.stack(g1R + g1I))
    g2rep = np.ascontiguousarray(np.stack(g2R + g2I))

    ident = np.eye(128, dtype=ml_dtypes.bfloat16)
    dgrid = (np.arange(DEC_W, dtype=np.float32)[None, :] - 384.0
             - np.arange(128, dtype=np.float32)[:, None]).astype(np.float32)
    hsel = np.zeros((128, 2), dtype=ml_dtypes.bfloat16)
    hsel[0:64, 0] = 1
    hsel[64:128, 1] = 1
    hselT = np.ones((2, 128), dtype=ml_dtypes.bfloat16)

    nc = _get_program()
    shared = {"wq": wq, "wk": wk, "wv": wv, "wo": wo, "bq": bq, "bk": bk,
              "bv": bv, "bo": bo, "bfn": bfn, "g1rep": g1rep, "g2rep": g2rep,
              "ident": ident, "dgrid": dgrid, "hsel": hsel, "hselT": hselT}
    in_maps = [dict(shared, x=np.ascontiguousarray(x[b])) for b in range(8)]
    res = run_bass_kernel_spmd(nc, in_maps, core_ids=list(range(8)))
    out = np.stack([np.asarray(r["out"], dtype=np.float32) for r in res.results])
    return out


# version 27 (final)
# speedup vs baseline: 1.1984x; 1.0065x over previous
import numpy as np
import ml_dtypes
from contextlib import ExitStack

import concourse.bass as bass
import concourse.mybir as mybir
from concourse.bass_utils import run_bass_kernel_spmd

F32 = mybir.dt.float32
BF16 = mybir.dt.bfloat16
AF = mybir.ActivationFunctionType
ALU = mybir.AluOpType
AX = mybir.AxisListType

SEQ, D, H, DK = 2048, 1024, 16, 64
NT = SEQ // 128
FT = D // 128
PAIRS = FT
NCH = SEQ // 512
DEC_W = 2432


class Sem:
    def __init__(self, stack, nc, name):
        self.h = stack.enter_context(nc.semaphore(name))
        self.v = 0


class Plan:
    def __init__(self):
        self.ops = {k: [] for k in ("sync", "gpsimd", "vector", "scalar", "tensor")}

    def emit(self, eng, build_fn, waits=(), inc=None, k=1):
        w = tuple((s.h, int(v)) for (s, v) in waits if s is not None)
        if inc is not None:
            inc.v += k
        def op(e, w=w, build_fn=build_fn, inc=inc, k=k):
            for (h, val) in w:
                e.wait_ge(h, val)
            ins = build_fn(e)
            if inc is not None:
                ins.then_inc(inc.h, k)
        self.ops[eng].append(op)
        return inc.v if inc is not None else None


class PsumPool:
    def __init__(self, tensors):
        self.t = tensors
        self.free_req = [None] * len(tensors)
        self.i = 0

    def acquire(self):
        j = self.i % len(self.t)
        self.i += 1
        req = self.free_req[j]
        return j, ([] if req is None else [req])

    def release(self, j, sem, val):
        self.free_req[j] = (sem, val)


DEBUG_TAPS = False
PARANOID = False
INTERLEAVE_QK = True
QUARTER_NORM = False


def build_program():
    nc = bass.Bass("TRN2", target_bir_lowering=False)

    dx = nc.dram_tensor("x", [SEQ, D], F32, kind="ExternalInput").ap()
    dwq = nc.dram_tensor("wq", [D, D], F32, kind="ExternalInput").ap()
    dwk = nc.dram_tensor("wk", [D, D], F32, kind="ExternalInput").ap()
    dwv = nc.dram_tensor("wv", [D, D], F32, kind="ExternalInput").ap()
    dwo = nc.dram_tensor("wo", [D, D], F32, kind="ExternalInput").ap()
    dbq = nc.dram_tensor("bq", [D], F32, kind="ExternalInput").ap()
    dbk = nc.dram_tensor("bk", [D], F32, kind="ExternalInput").ap()
    dbv = nc.dram_tensor("bv", [D], F32, kind="ExternalInput").ap()
    dbo = nc.dram_tensor("bo", [1, D], F32, kind="ExternalInput").ap()
    dbf = nc.dram_tensor("bfn", [1, D], F32, kind="ExternalInput").ap()
    dg1 = nc.dram_tensor("g1rep", [8, 512, 512], F32, kind="ExternalInput").ap()
    dg2 = nc.dram_tensor("g2rep", [8, 512, 512], F32, kind="ExternalInput").ap()
    dident = nc.dram_tensor("ident", [128, 128], BF16, kind="ExternalInput").ap()
    dgrid = nc.dram_tensor("dgrid", [128, DEC_W], F32, kind="ExternalInput").ap()
    dhsel = nc.dram_tensor("hsel", [128, 2], BF16, kind="ExternalInput").ap()
    dhselT = nc.dram_tensor("hselT", [2, 128], BF16, kind="ExternalInput").ap()
    dout = nc.dram_tensor("out", [SEQ, D], F32, kind="ExternalOutput").ap()
    taps = {}
    if DEBUG_TAPS:
        for nm, shp, dt in [("t_attn", [128, FT * SEQ], F32), ("t_x2", [128, NT * 1024], F32),
                            ("t_xnT", [128, FT * SEQ], F32), ("t_wf", [128, FT * D], F32),
                            ("t_decay", [128, DEC_W], F32), ("t_qt", [128, 2 * SEQ], F32),
                            ("t_kt", [128, 2 * SEQ], F32), ("t_negc", [128, 2], F32),
                            ("t_pt", [128, 2 * 1024], F32), ("t_pvpr", [128, 1024], F32),
                            ("t_v", [128, NT * 1024], F32)]:
            taps[nm] = nc.dram_tensor(nm, shp, dt, kind="ExternalOutput").ap()

    P = Plan()

    with ExitStack() as TOP:
        E = TOP.enter_context
        dS = Sem(TOP, nc, "dS")
        dG = Sem(TOP, nc, "dG")
        dG2 = Sem(TOP, nc, "dG2")
        dGrid = Sem(TOP, nc, "dGrid")
        dIdent = Sem(TOP, nc, "dIdent")
        dHsel = Sem(TOP, nc, "dHsel")
        dBq = Sem(TOP, nc, "dBq")
        dBk = Sem(TOP, nc, "dBk")
        dBo = Sem(TOP, nc, "dBo")
        dBf = Sem(TOP, nc, "dBf")
        dBv = Sem(TOP, nc, "dBv")
        dWv = Sem(TOP, nc, "dWv")
        dWo = Sem(TOP, nc, "dWo")
        dGrep = Sem(TOP, nc, "dGrep")
        dX = [Sem(TOP, nc, "dX0"), Sem(TOP, nc, "dX1"), Sem(TOP, nc, "dX2")]
        dXg = [Sem(TOP, nc, "dXg0"), Sem(TOP, nc, "dXg1"), Sem(TOP, nc, "dXg2")]
        tE = Sem(TOP, nc, "tE")
        vE = Sem(TOP, nc, "vE")
        aE = Sem(TOP, nc, "aE")
        pE = Sem(TOP, nc, "pE")

        attn_outT = E(nc.sbuf_tensor("attn_outT", [128, FT * SEQ], BF16))
        xnT = E(nc.sbuf_tensor("xnT", [128, FT * SEQ], BF16))
        xrot = E(nc.sbuf_tensor("xrot", [128, 3 * 1024], F32))
        xnrot = E(nc.sbuf_tensor("xnrot", [128, 2 * 1024], BF16))
        sqscr = E(nc.sbuf_tensor("sqscr", [128, 1024], F32))
        stat = E(nc.sbuf_tensor("stat", [128, 96], F32))
        identS = E(nc.sbuf_tensor("identS", [128, 128], BF16))
        ones1 = E(nc.sbuf_tensor("ones1", [1, 128], BF16))
        ones64 = E(nc.sbuf_tensor("ones64", [128, 64], BF16))
        hselS = E(nc.sbuf_tensor("hselS", [128, 2], BF16))
        hselTS = E(nc.sbuf_tensor("hselTS", [2, 128], BF16))
        bqS = E(nc.sbuf_tensor("bqS", [128, 8], F32))
        bq8S = E(nc.sbuf_tensor("bq8S", [128, 8], F32))
        bkS = E(nc.sbuf_tensor("bkS", [128, 8], F32))
        bvS = E(nc.sbuf_tensor("bvS", [128, 8], BF16))
        boHi = E(nc.sbuf_tensor("boHi", [1, D], BF16))
        boLo = E(nc.sbuf_tensor("boLo", [1, D], BF16))
        decayS = E(nc.sbuf_tensor("decayS", [128, DEC_W], BF16))

        def ftv(t):
            return t[:, :].rearrange("p (a n) -> p a n", a=FT)

        P.emit("gpsimd", lambda g: g.dma_start(
            out=bvS[:, :], in_=dbv.rearrange("(t p) -> p t", p=128),
            allow_slow_non_contiguous=True), inc=dBv, k=16)
        dg_bv = dBv.v

        maskscope = ExitStack()
        B1 = maskscope.enter_context(nc.sbuf_tensor("B1", [128, DEC_W], F32))
        B2 = maskscope.enter_context(nc.sbuf_tensor("B2", [128, DEC_W], F32))
        P.emit("sync", lambda s: s.dma_start(out=B1[:, :], in_=dgrid), inc=dGrid, k=16)
        ds_grid = dGrid.v
        P.emit("sync", lambda s: s.dma_start(out=identS[:, :], in_=dident), inc=dIdent, k=16)
        ds_ident = dIdent.v
        P.emit("sync", lambda s: s.dma_start(out=hselS[:, :], in_=dhsel), inc=dHsel, k=16)
        P.emit("sync", lambda s: s.dma_start(out=hselTS[:, :], in_=dhselT), inc=dHsel, k=16)
        ds_hsel = dHsel.v
        P.emit("sync", lambda s: s.dma_start(
            out=bqS[:, :], in_=dbq.rearrange("(t p) -> p t", p=128),
            allow_slow_non_contiguous=True), inc=dBq, k=16)
        ds_bq = dBq.v
        P.emit("sync", lambda s: s.dma_start(
            out=bkS[:, :], in_=dbk.rearrange("(t p) -> p t", p=128),
            allow_slow_non_contiguous=True), inc=dBk, k=16)
        ds_bk = dBk.v

        P.emit("vector", lambda v: v.memset(ones1[:, :], 1.0))
        P.emit("vector", lambda v: v.memset(ones64[:, :], 1.0))
        ve_ones = P.emit("vector", lambda v: v.drain(fusable=False), inc=vE)

        mask_state = {}

        def mask_step(step):
            if step == 0:
                m0 = P.emit("vector", lambda v: v.tensor_scalar(
                    B2[:, :], B1[:, :], 0.0, 1e30, ALU.min, ALU.mult),
                    waits=[(dGrid, ds_grid)], inc=vE)
                P.emit("vector", lambda v: v.tensor_scalar_max(
                    B1[:, :], B1[:, :], 1.0), waits=[(vE, m0)])
                mask_state["ve_b1"] = P.emit(
                    "vector", lambda v: v.drain(fusable=False), inc=vE)
            elif step == 1:
                P.emit("scalar", lambda s: s.activation(B1[:, :], B1[:, :], AF.Ln),
                       waits=[(vE, mask_state["ve_b1"])])
                mask_state["ae_ln0"] = P.emit(
                    "scalar", lambda s: s.drain(fusable=False), inc=aE)
            elif step == 2:
                P.emit("vector", lambda v: v.scalar_tensor_tensor(
                    B1[:, :], B1[:, :], -0.25, B2[:, :], ALU.mult, ALU.add),
                    waits=[(aE, mask_state["ae_ln0"])])
                mask_state["ve_strip"] = P.emit(
                    "vector", lambda v: v.drain(fusable=False), inc=vE)
            elif step == 3:
                P.emit("scalar", lambda s: s.activation(
                    decayS[:, :], B1[:, :], AF.Exp),
                    waits=[(vE, mask_state["ve_strip"])])
                mask_state["ae_decay"] = P.emit(
                    "scalar", lambda s: s.drain(fusable=False), inc=aE)

        P.emit("vector", lambda v: v.tensor_scalar_mul(bq8S[:, :], bqS[:, :], 0.125),
               waits=[(dBq, ds_bq)])
        ve_bq8 = P.emit("vector", lambda v: v.drain(fusable=False), inc=vE)
        maskscope.close()

        ln_state = {"prev_r2": None}

        def emit_ln_stats(i, src_ap, src_waits):
            c = i
            w0 = list(src_waits)
            P.emit("vector", lambda v, s=src_ap, c=c: v.reduce_sum(
                stat[:, c:c + 1], s, AX.X), waits=w0, inc=vE)
            sqw = list(src_waits)
            if ln_state["prev_r2"] is not None:
                sqw.append((vE, ln_state["prev_r2"]))
            sq = P.emit("gpsimd", lambda g, s=src_ap: g.tensor_mul(
                sqscr[:, :], s, s), waits=sqw, inc=pE)
            r2 = P.emit("vector", lambda v, c=c: v.reduce_sum(
                stat[:, 16 + c:17 + c], sqscr[:, :], AX.X),
                waits=[(pE, sq)], inc=vE)
            ln_state["prev_r2"] = r2
            nm = P.emit("vector", lambda v, c=c: v.tensor_scalar_mul(
                stat[:, 32 + c:33 + c], stat[:, c:c + 1], -1.0 / D),
                waits=[(vE, r2)], inc=vE)
            P.emit("vector", lambda v, c=c: v.tensor_scalar(
                stat[:, 48 + c:49 + c], stat[:, 16 + c:17 + c], 1.0 / D, 1e-5,
                ALU.mult, ALU.add), inc=vE)
            n2 = P.emit("vector", lambda v, c=c: v.tensor_scalar(
                stat[:, 64 + c:65 + c], stat[:, 32 + c:33 + c],
                stat[:, 32 + c:33 + c], -1.0, ALU.mult, ALU.mult),
                waits=[(vE, nm)], inc=vE)
            P.emit("vector", lambda v, c=c: v.tensor_add(
                stat[:, 48 + c:49 + c], stat[:, 48 + c:49 + c],
                stat[:, 64 + c:65 + c]), waits=[(vE, n2)])
            vr = P.emit("vector", lambda v: v.drain(fusable=False), inc=vE)
            return vr

        def emit_ln_lnact(i, vr):
            c = i
            P.emit("scalar", lambda s, c=c: s.activation(
                stat[:, 80 + c:81 + c], stat[:, 48 + c:49 + c], AF.Ln),
                waits=[(vE, vr)], inc=aE)
            ln_ = aE.v
            P.emit("scalar", lambda s, c=c: s.activation(
                stat[:, 80 + c:81 + c], stat[:, 80 + c:81 + c], AF.Exp, scale=-0.5),
                waits=[(aE, ln_)])
            rs = P.emit("scalar", lambda s: s.drain(fusable=False), inc=aE)
            return rs

        def emit_ln_apply(i, src_ap, rs, apply_extra):
            c = i
            sl = (i % 2) * 1024
            apw = [(aE, rs)] + list(apply_extra)
            P.emit("vector", lambda v, s=src_ap, c=c, sl=sl: v.tensor_scalar(
                xnrot[:, sl:sl + 1024], s, stat[:, 32 + c:33 + c],
                stat[:, 80 + c:81 + c], ALU.add, ALU.mult),
                waits=apw)
            ap = P.emit("vector", lambda v: v.drain(fusable=False), inc=vE)
            return ap

        xslot_free = [None, None, None]
        xnrot_tr_gate = [None, None]
        xe_rec = [0] * NT
        lnst = {}
        lnrs = {}
        with ExitStack() as PS1:
            psT = [PS1.enter_context(nc.psum_tensor("psT0", [128, 1024], BF16)),
                   PS1.enter_context(nc.psum_tensor("psT1", [128, 1024], BF16))]
            psT_free = [None, None]
            ds_x = {}
            xsems = {}
            for i in range(NT + 2):
                if i < NT:
                    sl3 = (i % 3) * 1024
                    wslot = [] if xslot_free[i % 3] is None else [xslot_free[i % 3]]
                    xeng = "gpsimd" if i >= 9 else "sync"
                    xsem = dXg[i % 3] if i >= 9 else dX[i % 3]
                    P.emit(xeng, lambda s, i=i, sl3=sl3: s.dma_start(
                        out=xrot[:, sl3:sl3 + 1024],
                        in_=dx[i * 128:(i + 1) * 128, :]),
                        waits=wslot, inc=xsem, k=16)
                    ds_x[i] = xsem.v
                    xsems[i] = xsem
                if 2 <= i and i - 2 < NT:
                    t = i - 2
                    sl3 = (t % 3) * 1024
                    sl = (t % 2) * 1024
                    extra = ([] if xnrot_tr_gate[t % 2] is None
                             else [xnrot_tr_gate[t % 2]])
                    ap = emit_ln_apply(t, xrot[:, sl3:sl3 + 1024], lnrs[t], extra)
                    xslot_free[t % 3] = (vE, ap)
                    tw = [(vE, ap), (dIdent, ds_ident)]
                    if psT_free[t % 2] is not None:
                        tw.append(psT_free[t % 2])
                    for j in range(FT):
                        P.emit("tensor", lambda t_, t=t, j=j, sl=sl: t_.transpose(
                            psT[t % 2][:, j * 128:(j + 1) * 128],
                            xnrot[:, sl + j * 128:sl + (j + 1) * 128], identS[:, :]),
                            waits=(tw if j == 0 else []),
                            inc=(tE if j == FT - 1 else None))
                    trrec = tE.v
                    xnrot_tr_gate[t % 2] = (tE, trrec)
                    dst = ftv(xnT)[:, :, t * 128:(t + 1) * 128]
                    srcT = psT[t % 2][:, :].rearrange("p (a n) -> p a n", a=FT)
                    xe = P.emit("vector", lambda v, dst=dst, srcT=srcT: v.tensor_copy(
                        dst, srcT), waits=[(tE, trrec)], inc=vE)
                    xe_rec[t] = xe
                    psT_free[t % 2] = (vE, xe)
                    if t in (2, 5, 8, 11):
                        mask_step((2, 5, 8, 11).index(t))
                if 1 <= i and i - 1 < NT:
                    lnrs[i - 1] = emit_ln_lnact(i - 1, lnst[i - 1])
                if i < NT:
                    lnst[i] = emit_ln_stats(
                        i, xrot[:, (i % 3) * 1024:(i % 3) * 1024 + 1024],
                        [(xsems[i], ds_x[i])])
        ae_decay = mask_state["ae_decay"]
        P.emit("vector", lambda v: v.drain(fusable=False), inc=vE)
        xnT_all = vE.v
        xnT_chunk = [(xe_rec[4 * c + 3 + 1] if 4 * c + 4 < NT else xnT_all)
                     for c in range(NCH)]

        WfS = TOP.enter_context(nc.sbuf_tensor("WfS", [128, FT * D], BF16))
        attnscope = ExitStack()
        QTs = attnscope.enter_context(nc.sbuf_tensor("QTs", [128, 2 * SEQ], BF16))
        KTs = attnscope.enter_context(nc.sbuf_tensor("KTs", [128, 2 * SEQ], BF16))
        V = attnscope.enter_context(nc.sbuf_tensor("V", [128, NT * 1024], BF16))
        PTb = attnscope.enter_context(nc.sbuf_tensor("PTb", [128, 4 * 1024], BF16))
        Wqp = attnscope.enter_context(nc.sbuf_tensor("Wqp", [128, 2 * 1024], BF16))
        Wkp = attnscope.enter_context(nc.sbuf_tensor("Wkp", [128, 2 * 1024], BF16))
        cmS = attnscope.enter_context(nc.sbuf_tensor("cmS", [2, 16], F32))
        cmbf = attnscope.enter_context(nc.sbuf_tensor("cmbf", [2, 1], BF16))
        negCS = attnscope.enter_context(nc.sbuf_tensor("negCS", [128, 2], F32))
        recipS = attnscope.enter_context(nc.sbuf_tensor("recipS", [128, 512], F32))
        qsqT = attnscope.enter_context(nc.sbuf_tensor("qsqT", [128, SEQ], BF16))
        qsqbf = qsqT[:, :]

        pscope = ExitStack()
        stP = [pscope.enter_context(nc.psum_tensor("stP0", [128, 1024], F32)),
               pscope.enter_context(nc.psum_tensor("stP1", [128, 1024], F32))]
        qkP = [pscope.enter_context(nc.psum_tensor("qk0", [128, 512], F32)),
               pscope.enter_context(nc.psum_tensor("qk1", [128, 512], F32))]
        pool = PsumPool(qkP)

        wslot_gate = [None, None]
        qkslot_gate = [(aE, ae_decay), (aE, ae_decay)]
        dg_w = [None] * PAIRS

        def emit_qk_weights(p):
            pi = p % 2
            wg = [] if wslot_gate[pi] is None else [wslot_gate[pi]]
            src_q = dwq.rearrange("(t p) f -> p t f", p=128)[:, :, p * 128:(p + 1) * 128]
            src_k = dwk.rearrange("(t p) f -> p t f", p=128)[:, :, p * 128:(p + 1) * 128]
            sem_w = dG if pi == 0 else dG2
            P.emit("gpsimd", lambda g, pi=pi, src_q=src_q: g.dma_start(
                out=Wqp[:, pi * 1024:(pi + 1) * 1024].rearrange(
                    "p (t f) -> p t f", t=FT), in_=src_q),
                waits=wg, inc=sem_w, k=16)
            P.emit("gpsimd", lambda g, pi=pi, src_k=src_k: g.dma_start(
                out=Wkp[:, pi * 1024:(pi + 1) * 1024].rearrange(
                    "p (t f) -> p t f", t=FT), in_=src_k),
                inc=sem_w, k=16)
            dg_w[p] = (sem_w, sem_w.v)

        def emit_qk_compute(p):
            pi = p % 2
            last_ev = 0
            last_mm = 0
            for (which, Wt) in (("q", Wqp), ("k", Wkp)):
                for c in range(NCH):
                    j, pw = pool.acquire()
                    for tin in range(FT):
                        w = []
                        if tin == 0:
                            w = [(vE, xnT_chunk[c]), dg_w[p]] + pw
                            if qkslot_gate[pi] is not None:
                                w.append(qkslot_gate[pi])
                        P.emit("tensor", lambda t, j=j, Wt=Wt, tin=tin, pi=pi, c=c:
                               t.matmul(qkP[j][:, 0:512],
                                        Wt[:, pi * 1024 + tin * 128:pi * 1024 + (tin + 1) * 128],
                                        ftv(xnT)[:, tin, c * 512:(c + 1) * 512],
                                        start=(tin == 0), stop=(tin == FT - 1)),
                               waits=w, inc=(tE if tin == FT - 1 else None))
                    last_mm = tE.v
                    dstT = QTs if which == "q" else KTs
                    scale = 0.125 if which == "q" else 1.0
                    bias = bq8S if which == "q" else bkS
                    er = P.emit("scalar", lambda s, dstT=dstT, pi=pi, c=c, j=j,
                                scale=scale, bias=bias, p=p:
                                s.activation(
                                    dstT[:, pi * SEQ + c * 512:pi * SEQ + (c + 1) * 512],
                                    qkP[j][:, 0:512], AF.Identity,
                                    bias=bias[:, p:p + 1], scale=scale),
                                waits=[(tE, last_mm), (dBk, ds_bk), (vE, ve_bq8)],
                                inc=aE)
                    last_ev = er
                    pool.release(j, aE, er)
                    yield
            wslot_gate[pi] = (tE, last_mm)
            red = 0
            for (kk, (which, srcT)) in enumerate((("q", QTs), ("k", KTs))):
                P.emit("vector", lambda v, srcT=srcT, pi=pi: v.tensor_mul(
                    qsqbf[:, 0:1024], srcT[:, pi * SEQ:pi * SEQ + 1024],
                    srcT[:, pi * SEQ:pi * SEQ + 1024]),
                    waits=[(aE, last_ev)])
                yield
                P.emit("vector", lambda v, srcT=srcT, pi=pi: v.tensor_mul(
                    qsqbf[:, 1024:2048],
                    srcT[:, pi * SEQ + 1024:pi * SEQ + 2048],
                    srcT[:, pi * SEQ + 1024:pi * SEQ + 2048]))
                sq = P.emit("vector", lambda v: v.drain(fusable=False), inc=vE)
                yield
                for c in range(NCH):
                    j, pw = pool.acquire()
                    mm = P.emit("tensor", lambda t, j=j, c=c: t.matmul(
                        qkP[j][0:2, 0:512], hselS[:, :],
                        qsqbf[:, c * 512:(c + 1) * 512], start=True, stop=True),
                        waits=[(vE, sq), (dHsel, ds_hsel)] + pw, inc=tE)
                    red = P.emit("vector", lambda v, j=j, cc=kk * 4 + c:
                                 v.reduce_max(cmS[:, cc:cc + 1],
                                              qkP[j][0:2, 0:512], AX.X),
                                 waits=[(tE, mm)], inc=vE)
                    pool.release(j, vE, red)
                    yield
            qm = P.emit("vector", lambda v: v.reduce_max(
                cmS[:, 8:9], cmS[:, 0:4], AX.X), waits=[(vE, red)], inc=vE)
            km = P.emit("vector", lambda v: v.reduce_max(
                cmS[:, 9:10], cmS[:, 4:8], AX.X), waits=[(vE, qm)], inc=vE)
            P.emit("vector", lambda v: v.tensor_mul(
                cmS[:, 10:11], cmS[:, 8:9], cmS[:, 9:10]), waits=[(vE, km)])
            pr_ = P.emit("vector", lambda v: v.drain(fusable=False), inc=vE)
            yield
            l1 = P.emit("scalar", lambda s: s.activation(
                cmS[:, 11:12], cmS[:, 10:11], AF.Ln), waits=[(vE, pr_)], inc=aE)
            P.emit("scalar", lambda s: s.activation(
                cmS[:, 11:12], cmS[:, 11:12], AF.Exp, scale=0.5),
                waits=[(aE, l1)])
            e1 = P.emit("scalar", lambda s: s.drain(fusable=False), inc=aE)
            yield
            P.emit("vector", lambda v: v.tensor_copy(cmbf[:, :], cmS[:, 11:12]),
                   waits=[(aE, e1)])
            cb = P.emit("vector", lambda v: v.drain(fusable=False), inc=vE)
            yield
            j, pw = pool.acquire()
            rm = P.emit("tensor", lambda t, j=j: t.matmul(
                qkP[j][:, 0:1], hselTS[:, :], cmbf[:, :], start=True, stop=True),
                waits=[(vE, cb)] + pw, inc=tE)
            ng = P.emit("scalar", lambda s, j=j, pi=pi: s.activation(
                negCS[:, pi:pi + 1], qkP[j][:, 0:1], AF.Copy, scale=-0.5),
                waits=[(tE, rm)], inc=aE)
            pool.release(j, aE, ng)
            negc[p] = ng
            if DEBUG_TAPS and p == PAIRS - 1:
                P.emit("gpsimd", lambda g: g.dma_start(
                    out=taps["t_qt"], in_=QTs[:, :]),
                    waits=[(aE, ng)], inc=dG, k=16)
                P.emit("gpsimd", lambda g: g.dma_start(
                    out=taps["t_kt"], in_=KTs[:, :]), inc=dG, k=16)
                P.emit("gpsimd", lambda g: g.dma_start(
                    out=taps["t_negc"], in_=negCS[:, :]), inc=dG, k=16)

        emit_qk_weights(0)
        emit_qk_weights(1)
        negc = [None] * PAIRS
        for _ in emit_qk_compute(0):
            pass

        v_done = 0
        with ExitStack() as WVS:
            WvS = WVS.enter_context(nc.sbuf_tensor("WvS", [128, FT * 512], BF16))
            for half in range(2):
                lastmm_for_dma = None
                if half == 1:
                    lastmm_for_dma = (tE, tE.v)
                P.emit("gpsimd", lambda g, half=half: g.dma_start(
                    out=WvS[:, :].rearrange("p (t f) -> p t f", t=FT),
                    in_=dwv.rearrange("(t p) f -> p t f", p=128)[:, :, half * 512:(half + 1) * 512]),
                    waits=([lastmm_for_dma] if lastmm_for_dma else []), inc=dWv, k=16)
                dg_wv = dWv.v
                for mt in range(NT):
                    j, pw = pool.acquire()
                    for tin in range(FT):
                        w = []
                        if tin == 0:
                            w = [(vE, xnT_all), (dWv, dg_wv)] + pw
                        P.emit("tensor", lambda t, j=j, mt=mt, half=half, tin=tin:
                               t.matmul(qkP[j][:, 0:512],
                                        ftv(xnT)[:, tin, mt * 128:(mt + 1) * 128],
                                        WvS[:, tin * 512:(tin + 1) * 512],
                                        start=(tin == 0), stop=(tin == FT - 1)),
                               waits=w, inc=(tE if tin == FT - 1 else None))
                    mmrec = tE.v
                    evw = [(tE, mmrec)]
                    if mt == 0 and half == 0:
                        evw.append((aE, ae_decay))
                    ev = P.emit("vector", lambda v, j=j, mt=mt, half=half:
                                v.tensor_copy(
                                    V[:, mt * 1024 + half * 512:mt * 1024 + (half + 1) * 512],
                                    qkP[j][:, 0:512]),
                                waits=evw, inc=vE)
                    v_done = ev
                    pool.release(j, vE, ev)
            P.emit("vector", lambda v: v.drain(fusable=False), inc=vE)
            v_done = vE.v
            v_mm_done = tE.v

        gscope = ExitStack()
        Wacc = gscope.enter_context(nc.sbuf_tensor("Wacc", [128, 2048], F32))
        g1b = gscope.enter_context(nc.sbuf_tensor("g1b", [128, 512], F32))
        g2b = gscope.enter_context(nc.sbuf_tensor("g2b", [128, 512], F32))

        wb_units = [(part, c, pt) for part in range(2) for c in range(4)
                    for pt in range(4)]
        wb_state = {"gfree": (tE, v_mm_done), "idx": 0, "done": None}

        def emit_wbuild_unit():
            if wb_state["idx"] >= len(wb_units):
                return
            part, c, pt = wb_units[wb_state["idx"]]
            wb_state["idx"] += 1
            gi = part * 4 + c
            wg = [] if wb_state["gfree"] is None else [wb_state["gfree"]]
            P.emit("sync", lambda s, gi=gi, pt=pt: s.dma_start(
                out=g1b[:, :], in_=dg1[gi, pt * 128:(pt + 1) * 128, :]),
                waits=wg, inc=dGrep, k=16)
            P.emit("sync", lambda s, gi=gi, pt=pt: s.dma_start(
                out=g2b[:, :], in_=dg2[gi, pt * 128:(pt + 1) * 128, :]),
                inc=dGrep, k=16)
            ds_g = dGrep.v
            if c == 0:
                u = P.emit("gpsimd", lambda g, pt=pt: g.tensor_mul(
                    Wacc[:, pt * 512:(pt + 1) * 512], g1b[:, :], g2b[:, :]),
                    waits=[(dGrep, ds_g)], inc=pE)
                wb_state["gfree"] = (pE, u)
            else:
                u1 = P.emit("gpsimd", lambda g: g.tensor_mul(
                    g1b[:, :], g1b[:, :], g2b[:, :]), waits=[(dGrep, ds_g)], inc=pE)
                u2 = P.emit("gpsimd", lambda g, pt=pt: g.tensor_add(
                    Wacc[:, pt * 512:(pt + 1) * 512],
                    Wacc[:, pt * 512:(pt + 1) * 512], g1b[:, :]),
                    waits=[(pE, u1)], inc=pE)
                wb_state["gfree"] = (pE, u2)
            if wb_state["idx"] == len(wb_units) // 2:
                last = 0
                for pt in range(4):
                    last = P.emit("gpsimd", lambda g, pt=pt: g.tensor_copy(
                        WfS[:, pt * D:pt * D + 512],
                        Wacc[:, pt * 512:(pt + 1) * 512]),
                        waits=([wb_state["gfree"]] if pt == 0 else []), inc=pE)
                    last = P.emit("gpsimd", lambda g, pt=pt: g.tensor_copy(
                        WfS[:, (4 + pt) * D + 512:(5 + pt) * D],
                        Wacc[:, pt * 512:(pt + 1) * 512]), inc=pE)
                wb_state["gfree"] = (pE, last)
            if wb_state["idx"] == len(wb_units):
                last = 0
                for pt in range(4):
                    last = P.emit("gpsimd", lambda g, pt=pt: g.tensor_copy(
                        WfS[:, pt * D + 512:(pt + 1) * D],
                        Wacc[:, pt * 512:(pt + 1) * 512]),
                        waits=([wb_state["gfree"]] if pt == 0 else []), inc=pE)
                    last = P.emit("gpsimd", lambda g, pt=pt: g.tensor_scalar_mul(
                        WfS[:, (4 + pt) * D:(4 + pt) * D + 512],
                        Wacc[:, pt * 512:(pt + 1) * 512], -1.0), inc=pE)
                last = P.emit("gpsimd", lambda g: g.drain(fusable=False), inc=pE)
                wb_state["done"] = last

        gstep = 0
        dbg_state = {}
        gav = {}
        exp_by_j = {0: None, 1: None}
        dg_wo = None
        with ExitStack() as AVS:
            pv = AVS.enter_context(nc.psum_tensor("pv0", [128, 512], F32))
            pr = AVS.enter_context(nc.psum_tensor("pr0", [128, 512], F32))
            aev_q = [None, None, None, None]
            pvpr_free = None
            aev_last = 0

            def gen_pair_filler(p):
                if p + 2 < PAIRS:
                    emit_qk_weights(p + 2)
                yield
                if p + 1 < PAIRS:
                    yield from emit_qk_compute(p + 1)
                for _ in range(5):
                    emit_wbuild_unit()
                    yield

            for p in range(PAIRS):
                pi = p % 2
                ng = negc[p]
                stp_wait = {0: ([(aE, exp_by_j[0])] if exp_by_j[0] else []),
                            1: ([(aE, exp_by_j[1])] if exp_by_j[1] else [])}
                steps = [(c, mt) for c in range(NCH) for mt in range(4 * c + 4)]
                S = len(steps)
                g0 = gstep
                strec = {}
                er = {}
                erh = {}
                mrr = {}

                def emit_ST(k):
                    c, mt = steps[k]
                    j = k % 2
                    no = max(0, 128 * mt - 512 * c)
                    if k < 2:
                        w = [(aE, ng)] + stp_wait[j]
                    elif last_av[0] < k - 2:
                        w = [(aE, er[k - 2])]
                    else:
                        w = []
                    P.emit("tensor", lambda t, j=j, pi=pi, mt=mt, c=c, no=no: t.matmul(
                        stP[j][:, no:512],
                        KTs[0:64, pi * SEQ + mt * 128:pi * SEQ + (mt + 1) * 128],
                        QTs[0:64, pi * SEQ + c * 512 + no:pi * SEQ + (c + 1) * 512],
                        start=True, stop=True, tile_position=(0, 0)),
                        waits=w)
                    P.emit("tensor", lambda t, j=j, pi=pi, mt=mt, c=c, no=no: t.matmul(
                        stP[j][:, 512 + no:1024],
                        KTs[64:128, pi * SEQ + mt * 128:pi * SEQ + (mt + 1) * 128],
                        QTs[64:128, pi * SEQ + c * 512 + no:pi * SEQ + (c + 1) * 512],
                        start=True, stop=True, tile_position=(64, 0)),
                        inc=(None if PARANOID else tE))
                    if PARANOID:
                        P.emit("tensor", lambda t: t.drain(fusable=False), inc=tE)
                    strec[k] = tE.v

                def emit_exp(k):
                    c, mt = steps[k]
                    j = k % 2
                    no = max(0, 128 * mt - 512 * c)
                    slot = (g0 + k) % 4
                    we = [(tE, strec[k])]
                    if (g0 + k - 4) in gav:
                        we.append((tE, gav[g0 + k - 4]))
                    if no == 0:
                        e_ = P.emit("scalar", lambda s, j=j, slot=slot, pi=pi:
                                    s.activation(
                                        PTb[:, slot * 1024:(slot + 1) * 1024],
                                        stP[j][:, 0:1024], AF.Exp,
                                        bias=negCS[:, pi:pi + 1]),
                                    waits=we, inc=aE)
                        eh = e_
                    else:
                        eh = P.emit("scalar", lambda s, j=j, slot=slot, pi=pi, no=no:
                                    s.activation(PTb[:, slot * 1024 + no:slot * 1024 + 512],
                                                 stP[j][:, no:512], AF.Exp,
                                                 bias=negCS[:, pi:pi + 1]),
                                    waits=we, inc=aE)
                        e_ = P.emit("scalar", lambda s, j=j, slot=slot, pi=pi, no=no:
                                    s.activation(PTb[:, slot * 1024 + 512 + no:(slot + 1) * 1024],
                                                 stP[j][:, 512 + no:1024], AF.Exp,
                                                 bias=negCS[:, pi:pi + 1]),
                                    inc=aE)
                    er[k] = e_
                    erh[k] = eh
                    stp_wait[j] = [(aE, e_)]
                    exp_by_j[j] = e_

                def emit_mult(k):
                    c, mt = steps[k]
                    slot = (g0 + k) % 4
                    no = max(0, 128 * mt - 512 * c)
                    off = 384 + c * 512 - mt * 128
                    P.emit("vector", lambda v, slot=slot, off=off, no=no: v.tensor_mul(
                        PTb[:, slot * 1024 + no:slot * 1024 + 512],
                        PTb[:, slot * 1024 + no:slot * 1024 + 512],
                        decayS[:, off + no:off + 512]),
                        waits=[(aE, erh[k]), (aE, ae_decay)])
                    m_ = P.emit("vector", lambda v, slot=slot, off=off, no=no: v.tensor_mul(
                        PTb[:, slot * 1024 + 512 + no:(slot + 1) * 1024],
                        PTb[:, slot * 1024 + 512 + no:(slot + 1) * 1024],
                        decayS[:, off + no:off + 512]),
                        waits=[(aE, er[k])], inc=vE)
                    mrr[k] = m_

                def emit_attnV(k, pgate=None):
                    nonlocal aev_last, pvpr_free
                    c, mt = steps[k]
                    nmt = 4 * c + 4
                    slot = (g0 + k) % 4
                    no = max(0, 128 * mt - 512 * c)
                    wv_ = [(vE, pgate if pgate is not None else mrr[k])]
                    if mt == 0:
                        wv_.append((vE, v_done))
                        wv_.append((vE, ve_ones))
                        if pvpr_free is not None:
                            wv_.append(pvpr_free)
                    h0, h1 = 2 * p, 2 * p + 1
                    islast = (mt == nmt - 1)
                    P.emit("tensor", lambda t, mt=mt, h0=h0, slot=slot,
                           islast=islast, no=no:
                           t.matmul(pv[0:64, no:512],
                                    V[:, mt * 1024 + h0 * 64:mt * 1024 + h0 * 64 + 64],
                                    PTb[:, slot * 1024 + no:slot * 1024 + 512],
                                    start=(mt == 0), stop=islast,
                                    tile_position=(0, 0), skip_group_check=True),
                           waits=wv_)
                    P.emit("tensor", lambda t, mt=mt, h1=h1, slot=slot,
                           islast=islast, no=no:
                           t.matmul(pv[64:128, no:512],
                                    V[:, mt * 1024 + h1 * 64:mt * 1024 + h1 * 64 + 64],
                                    PTb[:, slot * 1024 + 512 + no:(slot + 1) * 1024],
                                    start=(mt == 0), stop=islast,
                                    tile_position=(0, 64), skip_group_check=True))
                    P.emit("tensor", lambda t, mt=mt, slot=slot,
                           islast=islast, no=no:
                           t.matmul(pr[0:64, no:512], ones64[:, :],
                                    PTb[:, slot * 1024 + no:slot * 1024 + 512],
                                    start=(mt == 0), stop=islast,
                                    tile_position=(0, 0), skip_group_check=True))
                    P.emit("tensor", lambda t, mt=mt, slot=slot,
                           islast=islast, no=no:
                           t.matmul(pr[64:128, no:512], ones64[:, :],
                                    PTb[:, slot * 1024 + 512 + no:(slot + 1) * 1024],
                                    start=(mt == 0), stop=islast,
                                    tile_position=(0, 64), skip_group_check=True),
                           inc=(None if (PARANOID and islast) else tE))
                    if PARANOID and islast:
                        P.emit("tensor", lambda t: t.drain(fusable=False), inc=tE)
                    gav[g0 + k] = tE.v
                    if (QUARTER_NORM and mt >= 4 * c) or \
                       (not QUARTER_NORM and islast):
                        if QUARTER_NORM:
                            q = mt - 4 * c
                            lo, wd = 128 * q, 128
                        else:
                            q = 3
                            lo, wd = 0, 512
                        h = wd // 2
                        rcw = [(tE, gav[g0 + k])]
                        if aev_q[q] is not None:
                            rcw.append(aev_q[q])
                        rl = P.emit("scalar", lambda s, lo=lo, wd=wd: s.activation(
                            recipS[:, lo:lo + wd], pr[:, lo:lo + wd], AF.Ln),
                            waits=rcw, inc=aE)
                        ra = P.emit("scalar", lambda s, lo=lo, h=h: s.activation(
                            recipS[:, lo:lo + h], recipS[:, lo:lo + h],
                            AF.Exp, scale=-1.0), waits=[(aE, rl)], inc=aE)
                        rb = P.emit("scalar", lambda s, lo=lo, h=h: s.activation(
                            recipS[:, lo + h:lo + 2 * h],
                            recipS[:, lo + h:lo + 2 * h],
                            AF.Exp, scale=-1.0), inc=aE)
                        base = p * SEQ + c * 512 + lo
                        P.emit("vector", lambda v, lo=lo, base=base, h=h: v.tensor_mul(
                            attn_outT[:, base:base + h],
                            pv[:, lo:lo + h], recipS[:, lo:lo + h]),
                            waits=[(aE, ra)])
                        ae_ = P.emit("vector", lambda v, lo=lo, base=base, h=h: v.tensor_mul(
                            attn_outT[:, base + h:base + 2 * h],
                            pv[:, lo + h:lo + 2 * h], recipS[:, lo + h:lo + 2 * h]),
                            waits=[(aE, rb)], inc=vE)
                        aev_q[q] = (vE, ae_)
                        aev_last = ae_
                        if q == 3:
                            pvpr_free = (vE, ae_)
                    if islast and DEBUG_TAPS and p == PAIRS - 1 and c == 0:
                        d1 = P.emit("vector", lambda v: v.tensor_copy(
                            sqscr[:, 0:512], pv[:, :]),
                            waits=[(tE, gav[g0 + k])], inc=vE)
                        d2 = P.emit("vector", lambda v: v.tensor_copy(
                            sqscr[:, 512:1024], pr[:, :]),
                            waits=[(vE, d1)], inc=vE)
                        pvpr_free = (vE, d2)
                        P.emit("sync", lambda s: s.dma_start(
                            out=taps["t_pvpr"], in_=sqscr[:, :]),
                            waits=[(vE, d2)], inc=dS, k=16)
                        P.emit("gpsimd", lambda g: g.dma_start(
                            out=taps["t_v"], in_=V[:, :]), inc=dG, k=16)

                av_sched = {}
                prev_it = -1
                for kk in range(S):
                    mt0_ = steps[kk][1]
                    des = kk + (4 if mt0_ == 0 else 2)
                    it = max(des, prev_it)
                    prev_it = it
                    av_sched.setdefault(it, []).append(kk)
                last_av = [-1]
                filler = gen_pair_filler(p)
                for k in range(S):
                    if INTERLEAVE_QK:
                        next(filler, None)
                    for kk in av_sched.get(k, []):
                        emit_attnV(kk)
                        last_av[0] = kk
                    emit_ST(k)
                    emit_exp(k)
                    emit_mult(k)
                for it in range(S, prev_it + 1):
                    for kk in av_sched.get(it, []):
                        emit_attnV(kk)
                        last_av[0] = kk
                for _ in filler:
                    pass
                gstep = g0 + S
                qkslot_gate[pi] = (tE, gav[gstep - 1])
            attn_done_t = tE.v
            P.emit("vector", lambda v: v.drain(fusable=False),
                   waits=[(vE, aev_last)], inc=vE)
            attn_done_v = vE.v

        pscope.close()
        gscope.close()
        attnscope.close()

        WoS = TOP.enter_context(nc.sbuf_tensor("WoS", [128, FT * D], BF16))
        P.emit("gpsimd", lambda g: g.dma_start(
            out=WoS[:, :].rearrange("p (t f) -> p t f", t=FT),
            in_=dwo.rearrange("(t p) f -> p t f", p=128)),
            waits=[(tE, attn_done_t)], inc=dWo, k=16)
        dg_wo = dWo.v
        boS = TOP.enter_context(nc.sbuf_tensor("boS", [1, D], F32))
        bfHi = TOP.enter_context(nc.sbuf_tensor("bfHi", [1, D], BF16))
        bfLo = TOP.enter_context(nc.sbuf_tensor("bfLo", [1, D], BF16))
        with nc.sbuf_tensor("bfS", [1, D], F32) as bfS:
            P.emit("sync", lambda s: s.dma_start(out=bfS[:, :], in_=dbf),
                   waits=[(tE, attn_done_t)], inc=dBf, k=16)
            ds_bf = dBf.v
            P.emit("vector", lambda v, bfS=bfS: v.tensor_copy(bfHi[:, :], bfS[:, :]),
                   waits=[(dBf, ds_bf), (vE, attn_done_v)], inc=vE)
            ve_bfhi = vE.v
            P.emit("vector", lambda v, bfS=bfS: v.tensor_sub(
                bfLo[:, :], bfS[:, :], bfHi[:, :]), waits=[(vE, ve_bfhi)])
            ve_bflo = P.emit("vector", lambda v: v.drain(fusable=False), inc=vE)
        P.emit("sync", lambda s: s.dma_start(out=boS[:, :], in_=dbo),
               waits=[(tE, attn_done_t)], inc=dBo, k=16)
        ds_bo = dBo.v
        x2 = TOP.enter_context(nc.sbuf_tensor("x2", [128, NT * 1024], F32))
        outbuf = TOP.enter_context(nc.sbuf_tensor("outbuf", [128, 2 * 1024], F32))
        wf_done = wb_state["done"]

        with ExitStack() as PSB:
            with ExitStack() as BOS:
                psBO = BOS.enter_context(nc.psum_tensor("psBO", [1, 1024], F32))
                for ch in range(2):
                    for tin in range(FT):
                        w = []
                        if tin == 0 and ch == 0:
                            w = [(dWo, dg_wo), (dBv, dg_bv), (tE, attn_done_t)]
                        P.emit("tensor", lambda t, ch=ch, tin=tin: t.matmul(
                            psBO[0:1, ch * 512:(ch + 1) * 512], bvS[:, tin:tin + 1],
                            WoS[:, tin * D + ch * 512:tin * D + (ch + 1) * 512],
                            start=(tin == 0), stop=(tin == FT - 1)),
                            waits=w,
                            inc=(tE if (tin == FT - 1 and ch == 1) else None))
                bo_mm = tE.v
                b1_ = P.emit("vector", lambda v: v.tensor_add(
                    boS[:, :], psBO[0:1, :], boS[:, :]),
                    waits=[(tE, bo_mm), (dBo, ds_bo), (pE, wf_done),
                           (vE, attn_done_v)], inc=vE)
                b2_ = P.emit("vector", lambda v: v.tensor_copy(
                    boHi[:, :], boS[:, :]), waits=[(vE, b1_)], inc=vE)
                P.emit("vector", lambda v: v.tensor_sub(
                    boLo[:, :], boS[:, :], boHi[:, :]), waits=[(vE, b2_)])
                bo_ready = P.emit("vector", lambda v: v.drain(fusable=False), inc=vE)

            psW = [PSB.enter_context(nc.psum_tensor("psW0", [128, 512], F32)),
                   PSB.enter_context(nc.psum_tensor("psW1", [128, 512], F32))]
            psT2 = [PSB.enter_context(nc.psum_tensor("psT2a", [128, 1024], BF16)),
                    PSB.enter_context(nc.psum_tensor("psT2b", [128, 1024], BF16))]
            psF = [PSB.enter_context(nc.psum_tensor("psF0", [128, 1024], F32)),
                   PSB.enter_context(nc.psum_tensor("psF1", [128, 1024], F32))]
            wpool = PsumPool(psW)
            psT2_free = [None, None]
            psF_free = [None, None]
            resid1 = [0] * NT
            x2e_rec = [0] * NT
            out_dma = {}
            dSg = Sem(TOP, nc, "dSg")
            ffn_ev = [0] * NT

            def stageA(i):
                sl = (i % 2) * 1024
                wslot = [xslot_free[i % 2]] if xslot_free[i % 2] is not None else []
                P.emit("sync", lambda s, i=i, sl=sl: s.dma_start(
                    out=xrot[:, sl:sl + 1024], in_=dx[i * 128:(i + 1) * 128, :]),
                    waits=wslot, inc=dX[i % 2], k=16)
                ds_x2 = (dX[i % 2], dX[i % 2].v)
                for ch in range(2):
                    j, pw = wpool.acquire()
                    for tin in range(FT):
                        w = []
                        if tin == 0:
                            w = [(tE, attn_done_t), (vE, attn_done_v),
                                 (dWo, dg_wo), (vE, bo_ready)] + pw
                        P.emit("tensor", lambda t, j=j, i=i, ch=ch, tin=tin:
                               t.matmul(psW[j][:, :],
                                        attn_outT[:, tin * SEQ + i * 128:tin * SEQ + (i + 1) * 128],
                                        WoS[:, tin * D + ch * 512:tin * D + (ch + 1) * 512],
                                        start=(tin == 0), stop=False),
                               waits=w)
                    P.emit("tensor", lambda t, j=j, ch=ch: t.matmul(
                        psW[j][:, :], ones1[:, :],
                        boHi[0:1, ch * 512:(ch + 1) * 512],
                        start=False, stop=False))
                    P.emit("tensor", lambda t, j=j, ch=ch: t.matmul(
                        psW[j][:, :], ones1[:, :],
                        boLo[0:1, ch * 512:(ch + 1) * 512],
                        start=False, stop=True), inc=tE)
                    mmrec = tE.v
                    rv = P.emit("vector", lambda v, j=j, i=i, ch=ch, sl=sl:
                                v.tensor_add(
                                    x2[:, i * 1024 + ch * 512:i * 1024 + (ch + 1) * 512],
                                    psW[j][:, :],
                                    xrot[:, sl + ch * 512:sl + (ch + 1) * 512]),
                                waits=[(tE, mmrec), ds_x2], inc=vE)
                    wpool.release(j, vE, rv)
                    resid1[i] = rv
                xslot_free[i % 2] = (vE, resid1[i])

            ln2st = {}
            ln2rs = {}

            def stageB_stats(i):
                ln2st[i] = emit_ln_stats(
                    i, x2[:, i * 1024:(i + 1) * 1024], [(vE, resid1[i])])

            def stageB_ln(i):
                ln2rs[i] = emit_ln_lnact(i, ln2st[i])

            def stageB(i):
                sl = (i % 2) * 1024
                extra = [] if xnrot_tr_gate[i % 2] is None else [xnrot_tr_gate[i % 2]]
                ap = emit_ln_apply(i, x2[:, i * 1024:(i + 1) * 1024], ln2rs[i], extra)
                tw = [(vE, ap)]
                if psT2_free[i % 2] is not None:
                    tw.append(psT2_free[i % 2])
                for j in range(FT):
                    P.emit("tensor", lambda t, i=i, j=j, sl=sl: t.transpose(
                        psT2[i % 2][:, j * 128:(j + 1) * 128],
                        xnrot[:, sl + j * 128:sl + (j + 1) * 128], identS[:, :]),
                        waits=(tw if j == 0 else []),
                        inc=(tE if j == FT - 1 else None))
                trrec = tE.v
                xnrot_tr_gate[i % 2] = (tE, trrec)
                dst = ftv(xnT)[:, :, i * 128:(i + 1) * 128]
                srcT = psT2[i % 2][:, :].rearrange("p (a n) -> p a n", a=FT)
                xe = P.emit("vector", lambda v, dst=dst, srcT=srcT: v.tensor_copy(
                    dst, srcT), waits=[(tE, trrec)], inc=vE)
                x2e_rec[i] = xe
                psT2_free[i % 2] = (vE, xe)

            def stageC(i):
                j = i % 2
                for ch in range(2):
                    for tin in range(FT):
                        w = []
                        if tin == 0:
                            if ch == 0:
                                w = [(vE, x2e_rec[i]), (pE, wf_done)]
                                if psF_free[j] is not None:
                                    w.append(psF_free[j])
                        P.emit("tensor", lambda t, j=j, i=i, ch=ch, tin=tin:
                               t.matmul(psF[j][:, ch * 512:(ch + 1) * 512],
                                        ftv(xnT)[:, tin, i * 128:(i + 1) * 128],
                                        WfS[:, tin * D + ch * 512:tin * D + (ch + 1) * 512],
                                        start=(tin == 0), stop=False),
                               waits=w)
                    P.emit("tensor", lambda t, j=j, ch=ch: t.matmul(
                        psF[j][:, ch * 512:(ch + 1) * 512], ones1[:, :],
                        bfHi[0:1, ch * 512:(ch + 1) * 512],
                        start=False, stop=False),
                        waits=([(vE, ve_bflo)] if (i == 0 and ch == 0) else []))
                    P.emit("tensor", lambda t, j=j, ch=ch: t.matmul(
                        psF[j][:, ch * 512:(ch + 1) * 512], ones1[:, :],
                        bfLo[0:1, ch * 512:(ch + 1) * 512],
                        start=False, stop=True),
                        inc=(tE if ch == 1 else None))
                mmrec = tE.v
                wev = [(tE, mmrec)]
                if i >= 2 and (i // 2 - 1) in out_dma:
                    wev.append(out_dma[i // 2 - 1])
                ev = P.emit("vector", lambda v, j=j, i=i: v.tensor_add(
                    outbuf[:, (i % 2) * 1024:(i % 2 + 1) * 1024], psF[j][:, :],
                    x2[:, i * 1024:(i + 1) * 1024]),
                    waits=wev, inc=vE)
                ffn_ev[i] = ev
                psF_free[j] = (vE, ev)
                if i % 2 == 1:
                    g = i // 2
                    dst = dout[g * 256:(g + 1) * 256, :].rearrange(
                        "(t p) f -> p t f", p=128)
                    srcb = outbuf[:, :].rearrange("p (t f) -> p t f", t=2)
                    wdr = P.emit("vector", lambda v: v.drain(fusable=False),
                                 waits=[(vE, ffn_ev[i])], inc=vE)
                    qeng, qsem = (("sync", dS) if g % 2 == 0
                                  else ("gpsimd", dSg))
                    P.emit(qeng, lambda s, dst=dst, srcb=srcb: s.dma_start(
                        out=dst, in_=srcb), waits=[(vE, wdr)], inc=qsem, k=16)
                    out_dma[g] = (qsem, qsem.v)

            for i in range(NT + 4):
                if 4 <= i and i - 4 < NT:
                    stageC(i - 4)
                if 3 <= i and i - 3 < NT:
                    stageB(i - 3)
                if 2 <= i and i - 2 < NT:
                    stageB_ln(i - 2)
                if 1 <= i and i - 1 < NT:
                    stageB_stats(i - 1)
                if i < NT:
                    stageA(i)

            P.emit("sync", lambda s: s.nop(),
                   waits=[out_dma[NT // 2 - 2], out_dma[NT // 2 - 1]])
            if DEBUG_TAPS:
                P.emit("gpsimd", lambda g: g.dma_start(out=taps["t_attn"], in_=attn_outT[:, :]),
                       waits=[out_dma[NT // 2 - 2], out_dma[NT // 2 - 1]], inc=dG, k=16)
                P.emit("gpsimd", lambda g: g.dma_start(out=taps["t_xnT"], in_=xnT[:, :]), inc=dG, k=16)
                P.emit("gpsimd", lambda g: g.dma_start(out=taps["t_wf"], in_=WfS[:, :]), inc=dG, k=16)
                P.emit("gpsimd", lambda g: g.dma_start(out=taps["t_decay"], in_=decayS[:, :]), inc=dG, k=16)
                P.emit("sync", lambda s: s.dma_start(out=taps["t_x2"], in_=x2[:, :]), inc=dS, k=16)
                fin = dS.v
                P.emit("gpsimd", lambda g: g.nop(), waits=[(dG, dG.v), (dS, fin)])

        blk = TOP.enter_context(nc.Block())

        @blk.sync
        def _(e):
            for op in P.ops["sync"]:
                op(e)

        @blk.gpsimd
        def _(e):
            for op in P.ops["gpsimd"]:
                op(e)

        @blk.vector
        def _(e):
            for op in P.ops["vector"]:
                op(e)

        @blk.scalar
        def _(e):
            for op in P.ops["scalar"]:
                op(e)

        @blk.tensor
        def _(e):
            for op in P.ops["tensor"]:
                op(e)

    return nc


_NC_CACHE = {}


def _get_program():
    if "nc" not in _NC_CACHE:
        _NC_CACHE["nc"] = build_program()
    return _NC_CACHE["nc"]


def kernel(L0_attn_x, L0_attn_Wq, L0_attn_bq, L0_attn_Wk, L0_attn_bk,
           L0_attn_Wv, L0_attn_bv, L0_attn_Wo, L0_attn_bo,
           L0_ffn_G1_R, L0_ffn_G2_R, L0_ffn_G1_I, L0_ffn_G2_I,
           L0_ffn_bias_real, L0_ffn_bias_imag):
    x = np.ascontiguousarray(np.asarray(L0_attn_x, dtype=np.float32))
    wq = np.ascontiguousarray(np.asarray(L0_attn_Wq, dtype=np.float32))
    wk = np.ascontiguousarray(np.asarray(L0_attn_Wk, dtype=np.float32))
    wv = np.ascontiguousarray(np.asarray(L0_attn_Wv, dtype=np.float32))
    wo = np.ascontiguousarray(np.asarray(L0_attn_Wo, dtype=np.float32))
    bq = np.ascontiguousarray(np.asarray(L0_attn_bq, dtype=np.float32))
    bk = np.ascontiguousarray(np.asarray(L0_attn_bk, dtype=np.float32))
    bv = np.ascontiguousarray(np.asarray(L0_attn_bv, dtype=np.float32))
    bo = np.ascontiguousarray(np.asarray(L0_attn_bo, dtype=np.float32)).reshape(1, D)
    bfn = np.ascontiguousarray(np.concatenate([
        np.asarray(L0_ffn_bias_real, dtype=np.float32),
        np.asarray(L0_ffn_bias_imag, dtype=np.float32)]).reshape(1, D))

    def reps(G1, G2):
        g1l, g2l = [], []
        for c in range(4):
            X = np.asarray(G1, dtype=np.float32)[0, :, c, :]
            Y = np.asarray(G2, dtype=np.float32)[c, :, 0, :]
            g1l.append(np.ascontiguousarray(np.broadcast_to(
                X.T[:, None, :, None], (32, 16, 32, 16)).reshape(512, 512)))
            g2l.append(np.ascontiguousarray(np.broadcast_to(
                Y.T[None, :, None, :], (32, 16, 32, 16)).reshape(512, 512)))
        return g1l, g2l

    g1R, g2R = reps(L0_ffn_G1_R, L0_ffn_G2_R)
    g1I, g2I = reps(L0_ffn_G1_I, L0_ffn_G2_I)
    g1rep = np.ascontiguousarray(np.stack(g1R + g1I))
    g2rep = np.ascontiguousarray(np.stack(g2R + g2I))

    ident = np.eye(128, dtype=ml_dtypes.bfloat16)
    dgrid = (np.arange(DEC_W, dtype=np.float32)[None, :] - 384.0
             - np.arange(128, dtype=np.float32)[:, None]).astype(np.float32)
    hsel = np.zeros((128, 2), dtype=ml_dtypes.bfloat16)
    hsel[0:64, 0] = 1
    hsel[64:128, 1] = 1
    hselT = np.ones((2, 128), dtype=ml_dtypes.bfloat16)

    nc = _get_program()
    shared = {"wq": wq, "wk": wk, "wv": wv, "wo": wo, "bq": bq, "bk": bk,
              "bv": bv, "bo": bo, "bfn": bfn, "g1rep": g1rep, "g2rep": g2rep,
              "ident": ident, "dgrid": dgrid, "hsel": hsel, "hselT": hselT}
    in_maps = [dict(shared, x=np.ascontiguousarray(x[b])) for b in range(8)]
    res = run_bass_kernel_spmd(nc, in_maps, core_ids=list(range(8)))
    out = np.stack([np.asarray(r["out"], dtype=np.float32) for r in res.results])
    return out
